# version 31
# baseline (speedup 1.0000x reference)
import numpy as np
from contextlib import ExitStack
import concourse.bass as bass
import concourse.mybir as mybir
from concourse.bass import broadcast_tensor_aps
from concourse.bass_utils import run_bass_kernel_spmd

F32 = mybir.dt.float32
BF16 = mybir.dt.bfloat16
AF = mybir.ActivationFunctionType
ALU = mybir.AluOpType
AX = mybir.AxisListType

S, D, H, DH, LAT, NE, ED = 2048, 1024, 16, 64, 512, 8, 2048
NT = S // 128
KD = D // 128
KL = LAT // 128
KH = ED // 128
EPS_LN, EPS_PF, EPS_RMS = 1e-5, 1e-6, 1e-6

ENGS = ("t", "v", "s", "g", "y")


class Bld:

    def __init__(self):
        self.ops = {e: [] for e in ENGS}
        self.cnt = {e: 0 for e in ENGS}
        self.waited = {e: {} for e in ENGS}

    AUTOCHAIN = ("v", "s", "g")

    def emit(self, e, maker, inc=None, n=1):
        if e in self.AUTOCHAIN and inc is None:
            inc, n = e, 1
        if e in self.AUTOCHAIN and inc == e:
            prev = self.cnt.get(e, 0)
            if prev > 0:
                self.wait(e, (e, prev), force=True)
        tok = None
        if inc is not None:
            self.cnt[inc] = self.cnt.get(inc, 0) + n
            tok = (inc, self.cnt[inc])
        self.ops[e].append((maker, inc, n))
        return tok

    def wait(self, e, tok, force=False):
        if tok is None:
            return
        sem, val = tok
        if sem == e and not force:
            return
        if self.waited[e].get(sem, 0) >= val:
            return
        self.waited[e][sem] = val
        self.ops[e].append((("w", sem, val), None, 0))

    def waits(self, e, toks):
        for t in toks:
            self.wait(e, t)

    def replay(self, eng, name, sems):
        for maker, inc, n in self.ops[name]:
            if isinstance(maker, tuple):
                eng.wait_ge(sems[maker[1]], maker[2])
            else:
                ins = maker(eng)
                if inc is not None:
                    ins.then_inc(sems[inc], n)


def build(upto="full"):
    nc = bass.Bass("TRN2", target_bir_lowering=False)

    xin = nc.dram_tensor("x", [S, D], F32, kind="ExternalInput").ap()
    out = nc.dram_tensor("out", [S, D], F32, kind="ExternalOutput").ap()
    wd = {}
    for name, shape in [
        ("aWq", [D, D]), ("aWk", [D, D]), ("aWv", [D, D]), ("aWo", [D, D]),
        ("g1", [D]), ("fWq", [D, D]), ("fWdkv", [D, LAT]), ("fWuk", [LAT, D]),
        ("fWuv", [LAT, D]), ("fWo", [D, D]), ("g2", [D]), ("gw", [D, NE]),
        ("w1", [NE, D, ED]), ("w2", [NE, ED, D]),
    ]:
        wd[name] = nc.dram_tensor(name, shape, F32, kind="ExternalInput").ap()

    dbg = {}
    if upto == "B":
        dbg["zdbg"] = nc.dram_tensor("zdbg", [128, 16384], BF16, kind="ExternalOutput").ap()
        dbg["gdbg"] = nc.dram_tensor("gdbg", [128, 128], F32, kind="ExternalOutput").ap()

    accT = nc.alloc_sbuf_tensor("accT", [128, NT * 1024], F32)
    N1 = nc.alloc_sbuf_tensor("N1", [128, 16384], BF16)
    N2 = nc.alloc_sbuf_tensor("N2", [128, 16640], BF16)
    N3 = nc.alloc_sbuf_tensor("N3", [128, 16384], BF16)
    A4 = nc.alloc_sbuf_tensor("A4", [128, 8192], BF16)
    WSb = nc.alloc_sbuf_tensor("WSb", [128, 4096], BF16)
    stg1 = nc.alloc_sbuf_tensor("stg1", [128, 2048], BF16)
    stg2 = nc.alloc_sbuf_tensor("stg2", [128, 2080], BF16)
    PTb = nc.alloc_sbuf_tensor("PTb", [128, 2048], BF16)
    g2b = nc.alloc_sbuf_tensor("g2b", [128, 1024], BF16)
    g1k = nc.alloc_sbuf_tensor("g1k", [128, 8], F32)
    gwb = nc.alloc_sbuf_tensor("gwb", [128, 64], BF16)
    idn = nc.alloc_sbuf_tensor("idn", [128, 128], BF16)
    scrF = nc.alloc_sbuf_tensor("scrF", [128, 1024], F32)
    qpad = nc.alloc_sbuf_tensor("qpad", [128, 1024], BF16)
    stats = nc.alloc_sbuf_tensor("stats", [128, 72], F32)
    gts = nc.alloc_sbuf_tensor("gts", [128, 128], F32)
    cst = nc.alloc_sbuf_tensor("cst", [128, 8], F32)

    CONSTS = [EPS_LN, EPS_RMS, 0.0, 1.0, -1.0, 1.0 / 1024, -1e30, EPS_PF]
    for i, v in enumerate(CONSTS):
        nc.const_aps.register(F32, v, cst.ap()[:, i:i + 1])

    PSQ = nc.alloc_psum_tensor("PSQ", [128, 2048], F32)
    PS4 = nc.alloc_psum_tensor("PS4", [128, 512], F32)
    PS5 = nc.alloc_psum_tensor("PS5", [128, 512], F32)
    PST = nc.alloc_psum_tensor("PST", [128, 1024], BF16)
    PS7 = nc.alloc_psum_tensor("PS7", [128, 512], F32)
    QB = [PSQ.ap()[:, i * 512:(i + 1) * 512] for i in range(4)]
    SB = [PS4.ap(), PS5.ap()]
    S7 = PS7.ap()
    PSTa = PST.ap()
    PSTr = PST.ap().rearrange("p (k n) -> p k n", n=128)

    b = Bld()
    bank_free = [None] * 8

    def mk_memsets(aps_vals):
        def f(e):
            ins = None
            for a, v in aps_vals:
                ins = e.memset(a, v)
            return ins
        return f

    tok_cst = b.emit("v", mk_memsets([(cst.ap()[:, i:i + 1], v) for i, v in enumerate(CONSTS)]), inc="v")
    b.wait("s", tok_cst)
    tok_ms = b.emit("g", lambda e: e.memset(idn.ap(), 1.0), inc="g")
    b.wait("g", tok_ms, force=True)
    tok_idn = b.emit("g", lambda e: e.affine_select(
        out=idn.ap(), in_=idn.ap(), pattern=[[1, 128]],
        compare_op=ALU.is_equal, fill=0.0, base=0, channel_multiplier=-1), inc="g")
    tok_vones = []
    for half in range(2):
        ap = stg2.ap()[:, half * 1040:half * 1040 + 1040].rearrange(
            "p (h c) -> p h c", c=65)[:, :, 64:65]
        tok_vones.append(b.emit("v", lambda e, a=ap: e.memset(a, 1.0), inc="v"))

    def dma(eng, sem, dst, src, waits=(), slow=False):
        b.waits(eng, waits)
        return b.emit(eng, lambda e, d=dst, s=src, sl=slow: e.dma_start(
            out=d, in_=s, allow_slow_non_contiguous=sl), inc=sem, n=16)

    tok_xc = []
    for xc in range(4):
        tok_xc.append(dma("y", "dX%d" % xc,
                      accT.ap().rearrange("p (t c) -> p t c", c=1024)[:, xc * 4:(xc + 1) * 4, :],
                      xin.rearrange("(t p) c -> p t c", p=128)[:, xc * 4:(xc + 1) * 4, :]))
    tok_x = tok_xc[-1]

    N3r = N3.ap().rearrange("p (k n) -> p k n", n=1024)
    A4r = A4.ap()[:, 0:8192].rearrange("p (k n) -> p k n", n=1024)
    tok_wk = dma("g", "dWk", N3r[:, 0:8, :], wd["aWk"].rearrange("(k p) n -> p k n", p=128))
    tok_wv = dma("g", "dWv", N3r[:, 8:16, :], wd["aWv"].rearrange("(k p) n -> p k n", p=128))
    tok_wq = dma("g", "dWq", A4r, wd["aWq"].rearrange("(k p) n -> p k n", p=128))
    tok_g1 = dma("g", "dG1", g1k.ap(), wd["g1"].rearrange("(k p) -> p k", p=128), slow=True)
    tok_g2 = dma("g", "dG2", g2b.ap(),
                 wd["g2"].rearrange("(a n) -> a n", a=1).partition_broadcast(128))
    tok_gw = dma("g", "dGw", gwb.ap().rearrange("p (k n) -> p k n", n=8),
                 wd["gw"].rearrange("(k p) n -> p k n", p=128))
    tok_uv = dma("g", "dWuv", WSb.ap().rearrange("p (k n) -> p k n", n=1024),
                 wd["fWuv"].rearrange("(k p) n -> p k n", p=128))

    ln_guard = [None]

    def ln_stats(src_ap, wait_t=None):
        st = stats.ap()
        b.emit("v", lambda e: e.reduce_sum(out=st[:, 0:1], in_=src_ap, axis=AX.X))
        b.emit("v", lambda e: e.tensor_scalar_mul(out=st[:, 1:2], in0=st[:, 0:1], scalar1=1.0 / 1024))
        b.wait("s", ln_guard[0])
        if wait_t is not None:
            b.wait("s", wait_t)
        tsq = b.emit("s", lambda e: e.activation(out=scrF.ap(), in_=src_ap, func=AF.Square,
                                                 accum_out=st[:, 2:3]), inc="s")
        b.wait("v", tsq)
        b.emit("v", lambda e: e.tensor_scalar_mul(out=st[:, 2:3], in0=st[:, 2:3], scalar1=1.0 / 1024))
        b.emit("v", lambda e: e.tensor_mul(out=st[:, 3:4], in0=st[:, 1:2], in1=st[:, 1:2]))
        tv = b.emit("v", lambda e: e.tensor_sub(out=st[:, 3:4], in0=st[:, 2:3], in1=st[:, 3:4]), inc="v")
        b.wait("s", tv)
        b.wait("s", tok_cst)
        tsd = b.emit("s", lambda e: e.activation(out=st[:, 4:5], in_=st[:, 3:4], func=AF.Sqrt, bias=EPS_LN), inc="s")
        b.wait("v", tsd)
        trs = b.emit("v", lambda e: e.reciprocal(out=st[:, 5:6], in_=st[:, 4:5]), inc="v")
        ln_guard[0] = trs
        return trs

    def transpose_tile(src16, dst_ap, waits, eng="v"):
        b.waits("t", waits)
        b.wait("t", bank_free[6])
        tok = None
        for c in range(8):
            tok = b.emit("t", lambda e, cc=c: e.matmul(
                out=PSTa[:, cc * 128:(cc + 1) * 128], lhsT=src16[:, cc * 128:(cc + 1) * 128],
                rhs=idn.ap(), is_transpose=True, start=(cc == 0), stop=(cc == 7)),
                inc="t" if c == 7 else None)
        b.wait(eng, tok)
        if eng == "v":
            ev = b.emit("v", lambda e, d=dst_ap: e.tensor_copy(out=d, in_=PSTr), inc="v")
        else:
            ev = b.emit("s", lambda e, d=dst_ap: e.activation(out=d, in_=PSTr, func=AF.Copy), inc="s")
        bank_free[6] = ev
        return ev, tok

    N1v = N1.ap().rearrange("p (k n) -> p k n", n=2048)

    b.wait("t", tok_idn)
    scr_guard = [None, None]
    slot_guard = [None, None]
    tok_aFM = None
    st = stats.ap()
    b.wait("s", tok_cst)
    pend_tr = None
    for c in range(4):
        lo = 4 * c
        b.wait("v", tok_xc[c])
        b.wait("s", tok_xc[c])
        tsq = None
        for t in range(lo, lo + 4):
            acc_t = accT.ap()[:, t * 1024:(t + 1) * 1024]
            b.emit("s", lambda e, a=acc_t, tt=t: e.activation(
                out=scrF.ap(), in_=a, func=AF.Copy, accum_out=st[:, 16 + tt:17 + tt]))
            tsq = b.emit("s", lambda e, a=acc_t, tt=t: e.activation(
                out=scrF.ap(), in_=a, func=AF.Square, accum_out=st[:, 32 + tt:33 + tt]), inc="s")
        b.wait("v", tsq)
        b.emit("v", lambda e, l=lo: e.tensor_scalar_mul(out=st[:, 16 + l:20 + l], in0=st[:, 16 + l:20 + l], scalar1=1.0 / 1024))
        b.emit("v", lambda e, l=lo: e.tensor_scalar_mul(out=st[:, 32 + l:36 + l], in0=st[:, 32 + l:36 + l], scalar1=1.0 / 1024))
        b.emit("v", lambda e, l=lo: e.tensor_mul(out=st[:, l:l + 4], in0=st[:, 16 + l:20 + l], in1=st[:, 16 + l:20 + l]))
        tv = b.emit("v", lambda e, l=lo: e.tensor_sub(out=st[:, 32 + l:36 + l], in0=st[:, 32 + l:36 + l], in1=st[:, l:l + 4]), inc="v")
        b.wait("s", tv)
        tsd = b.emit("s", lambda e, l=lo: e.activation(out=st[:, 32 + l:36 + l], in_=st[:, 32 + l:36 + l], func=AF.Sqrt, bias=EPS_LN), inc="s")
        b.wait("v", tsd)
        b.emit("v", lambda e, l=lo: e.reciprocal(out=st[:, 32 + l:36 + l], in_=st[:, 32 + l:36 + l]))
        for t in range(lo, lo + 4):
            acc_t = accT.ap()[:, t * 1024:(t + 1) * 1024]
            slot = stg1.ap()[:, (t % 2) * 1024:(t % 2) * 1024 + 1024]
            b.wait("v", slot_guard[t % 2])
            ta = b.emit("v", lambda e, a=acc_t, sl=slot, tt=t: e.tensor_scalar(
                out=sl, in0=a, scalar1=st[:, 16 + tt:17 + tt], scalar2=st[:, 32 + tt:33 + tt],
                op0=ALU.subtract, op1=ALU.mult), inc="v")
            if pend_tr is not None:
                ev, mm = transpose_tile(*pend_tr[0:3])
                slot_guard[pend_tr[3] % 2] = mm
            pend_tr = (slot, N1v[:, 0:8, t * 128:(t + 1) * 128], [ta], t)
    ev, mm = transpose_tile(*pend_tr[0:3])
    tok_aFM = ev

    KVB = [SB[0], SB[1], S7]

    def kv_view(h):
        return KVB[h // 7][0:64, 65 * (h % 7):65 * (h % 7) + 65]

    kf_slot_free = [slot_guard[0], slot_guard[1]]
    v_slot_free = [tok_vones[0], tok_vones[1]]
    kv_last = None
    last_kproj_mm = None
    pend_kv = None

    def emit_kv(tt, kfa, vpa, tks):
        nonlocal kv_last
        b.waits("t", tks)
        for h in range(H):
            first = (tt == 0 and h % 7 == 0)
            last = (tt == NT - 1 and h in (6, 13, 15))
            kv_last = b.emit("t", lambda e, hh=h, kf_=kfa, vp_=vpa, st_=first, sp=last: e.matmul(
                out=kv_view(hh), lhsT=kf_[:, hh * 64:(hh + 1) * 64],
                rhs=vp_[:, hh * 65:(hh + 1) * 65], start=st_, stop=sp),
                inc="t" if h == H - 1 else None)
        kf_slot_free[tt % 2] = kv_last
        v_slot_free[tt % 2] = kv_last

    for tt in range(NT):
        b.waits("t", [tok_aFM, tok_wk, tok_wv])
        for bk in range(4):
            b.wait("t", bank_free[bk])
        tok_kv = None
        for k in range(KD):
            aT = N1v[:, k, tt * 128:(tt + 1) * 128]
            for j, (rhs, bk) in enumerate([
                (N3r[:, k, 0:512], 0), (N3r[:, k, 512:1024], 1),
                (N3r[:, 8 + k, 0:512], 2), (N3r[:, 8 + k, 512:1024], 3),
            ]):
                inc = "t" if (k == KD - 1 and j == 3) else None
                tok_kv = b.emit("t", lambda e, a=aT, r=rhs, bb=bk, st=(k == 0), sp=(k == KD - 1): e.matmul(
                    out=QB[bb], lhsT=a, rhs=r, start=st, stop=sp), inc=inc) or tok_kv
        last_kproj_mm = tok_kv
        if pend_kv is not None:
            emit_kv(*pend_kv)
            pend_kv = None
        sl = tt % 2
        kf = stg1.ap()[:, sl * 1024:sl * 1024 + 1024]
        vp = stg2.ap()[:, sl * 1040:sl * 1040 + 1040]
        b.wait("v", tok_kv)
        b.wait("v", kf_slot_free[sl])
        kpsum = PSQ.ap()[:, 0:1024]
        b.wait("v", scr_guard[0])
        b.emit("v", lambda e, kp=kpsum, d=kf: e.tensor_scalar_max(out=d[:, 0:512], in0=kp[:, 0:512], scalar1=0.0))
        tmin0 = b.emit("v", lambda e, kp=kpsum: e.tensor_scalar_min(out=scrF.ap()[:, 0:512], in0=kp[:, 0:512], scalar1=0.0), inc="v")
        b.emit("v", lambda e, kp=kpsum, d=kf: e.tensor_scalar_max(out=d[:, 512:1024], in0=kp[:, 512:1024], scalar1=0.0))
        tmin = b.emit("v", lambda e, kp=kpsum: e.tensor_scalar_min(out=scrF.ap()[:, 512:1024], in0=kp[:, 512:1024], scalar1=0.0), inc="v")
        vdst = vp.rearrange("p (h c) -> p h c", c=65)[:, :, 0:64]
        vsrc = PSQ.ap()[:, 1024:2048].rearrange("p (h c) -> p h c", c=64)
        b.wait("s", tok_kv)
        b.wait("s", v_slot_free[sl])
        tv = b.emit("s", lambda e, d=vdst, s_=vsrc: e.activation(out=d, in_=s_, func=AF.Copy), inc="s")
        b.wait("s", tmin)
        b.wait("s", scr_guard[0])
        te = b.emit("s", lambda e: e.activation(out=scrF.ap(), in_=scrF.ap(), func=AF.Exp), inc="s")
        bank_free[0] = tmin0
        bank_free[1] = tmin
        bank_free[2] = bank_free[3] = tv
        b.wait("g", te)
        tk = b.emit("g", lambda e, d=kf: e.tensor_add(out=d, in0=d, in1=scrF.ap()), inc="g")
        scr_guard[0] = tk
        pend_kv = (tt, kf, vp, [tk, tv])
    emit_kv(*pend_kv)
    kvS = PTb.ap()
    kstg = stg1.ap()[0:64, 0:1040]
    b.wait("v", kv_last)
    b.emit("v", lambda e: e.tensor_copy(out=kstg[:, 0:455], in_=SB[0][0:64, 0:455]))
    b.emit("v", lambda e: e.tensor_copy(out=kstg[:, 455:910], in_=SB[1][0:64, 0:455]))
    b.emit("v", lambda e: e.tensor_copy(out=kstg[:, 910:1040], in_=S7[0:64, 0:130]))
    tok_kvSv = b.emit("v", lambda e: e.memset(kvS[:, 0:1040], 0.0), inc="v")
    bank_free[4] = bank_free[5] = bank_free[7] = tok_kvSv
    bd = lambda a: a.rearrange("p (j c) -> p j c", c=130)
    dma("y", "dKV", bd(kvS[0:64, 0:1040])[:, :, 0:65],
        bd(kstg)[:, :, 0:65], waits=[tok_kvSv])
    tok_kvS = dma("y", "dKV", bd(kvS[64:128, 0:1040])[:, :, 65:130],
                  bd(kstg)[:, :, 65:130])

    if upto == "A2":
        b.waits("y", [tok_kvS])
        tok_out = dma("y", "dOut", out.rearrange("(t p) c -> p t c", p=128),
                      accT.ap().rearrange("p (t c) -> p t c", c=1024))
        b.wait("y", tok_out)
        return finish(nc, b)

    tok_wo = dma("g", "dWo", N3r[:, 0:8, :], wd["aWo"].rearrange("(k p) n -> p k n", p=128),
                 waits=[last_kproj_mm])

    N2q = N2.ap()[:, 0:16384].rearrange("p (m n) -> p m n", n=2048)
    tok_qf = None
    last_qproj_mm = None
    qi = 0
    for m in range(KD):
        for tg in range(4):
            bk = qi % 4
            qi += 1
            b.waits("t", [tok_wq, tok_aFM])
            b.wait("t", bank_free[bk])
            tmm = None
            for k in range(KD):
                tmm = b.emit("t", lambda e, kk=k, mm=m, tgg=tg, bb=bk, st=(k == 0), sp=(k == KD - 1): e.matmul(
                    out=QB[bb], lhsT=A4r[:, kk, mm * 128:(mm + 1) * 128],
                    rhs=N1v[:, kk, tgg * 512:(tgg + 1) * 512], start=st, stop=sp),
                    inc="t" if k == KD - 1 else None) or tmm
            last_qproj_mm = tmm
            b.wait("v", tmm)
            sc = scrF.ap()[:, (qi % 2) * 512:(qi % 2) * 512 + 512]
            dst = N2q[:, m, tg * 512:(tg + 1) * 512]
            b.emit("v", lambda e, bb=bk, d=dst: e.tensor_scalar_max(out=d, in0=QB[bb], scalar1=0.0))
            b.wait("v", scr_guard[qi % 2])
            tmin = b.emit("v", lambda e, bb=bk, s=sc: e.tensor_scalar_min(out=s, in0=QB[bb], scalar1=0.0), inc="v")
            b.wait("s", tmin)
            b.wait("s", scr_guard[qi % 2])
            te = b.emit("s", lambda e, s=sc: e.activation(out=s, in_=s, func=AF.Exp), inc="s")
            b.wait("g", te)
            tok_qf = b.emit("g", lambda e, d=dst, s=sc: e.tensor_add(out=d, in0=d, in1=s), inc="g")
            bank_free[bk] = tmin
            scr_guard[qi % 2] = tok_qf

    if upto == "A3":
        b.waits("y", [tok_qf])
        tok_out = dma("y", "dOut", out.rearrange("(t p) c -> p t c", p=128),
                      accT.ap().rearrange("p (t c) -> p t c", c=1024))
        b.wait("y", tok_out)
        return finish(nc, b)



    wo_done = []
    tok_num_last = None
    pend_a4 = None
    prev_scale_tok = [None]

    def emit_a4_tail(qt, attn, attn_toks):
        nonlocal wo_done, b1_tsq
        afm = stg2.ap()[:, 0:1024]
        afm_r = afm.rearrange("p (k n) -> p k n", n=128)
        ev, _ = transpose_tile(attn, afm_r, attn_toks, eng="s")
        evs = [ev]
        for half in range(2):
            bk = (qt % 2) * 2 + half
            b.waits("t", evs + [tok_wo])
            b.wait("t", bank_free[bk])
            tw = None
            for k in range(KD):
                tw = b.emit("t", lambda e, kk=k, hf=half, bb=bk, ar=afm_r, st=(k == 0), sp=(k == KD - 1): e.matmul(
                    out=QB[bb], lhsT=ar[:, kk, :], rhs=N3r[:, kk, hf * 512:(hf + 1) * 512],
                    start=st, stop=sp), inc="t" if k == KD - 1 else None) or tw
            b.wait("v", tw)
            dstp = accT.ap()[:, qt * 1024 + half * 512:qt * 1024 + (half + 1) * 512]
            tr1 = b.emit("v", lambda e, bb=bk, d=dstp: e.tensor_add(out=d, in0=QB[bb], in1=d), inc="v")
            bank_free[bk] = tr1
            wo_done.append(tr1)
        a_t = accT.ap()[:, qt * 1024:(qt + 1) * 1024]
        stp = stats.ap()
        b.emit("v", lambda e, a=a_t, tt=qt: e.reduce_sum(out=stp[:, 16 + tt:17 + tt], in_=a, axis=AX.X))
        b.wait("s", scr_guard[0])
        b.wait("s", scr_guard[1])
        b.wait("s", tr1)
        b1_tsq = b.emit("s", lambda e, a=a_t, tt=qt: e.activation(
            out=scrF.ap(), in_=a, func=AF.Square, accum_out=stp[:, 32 + tt:33 + tt]), inc="s")
        if qt == NT - 1:
            emit_b1_chain()

    def emit_b1_chain():
        st = stats.ap()
        b.wait("v", b1_tsq)
        b.emit("v", lambda e: e.tensor_scalar_mul(out=st[:, 16:32], in0=st[:, 16:32], scalar1=1.0 / 1024))
        b.emit("v", lambda e: e.tensor_scalar_mul(out=st[:, 32:48], in0=st[:, 32:48], scalar1=1.0 / 1024))
        b.emit("v", lambda e: e.tensor_mul(out=st[:, 0:16], in0=st[:, 16:32], in1=st[:, 16:32]))
        tv = b.emit("v", lambda e: e.tensor_sub(out=st[:, 32:48], in0=st[:, 32:48], in1=st[:, 0:16]), inc="v")
        b.wait("s", tv)
        tsd = b.emit("s", lambda e: e.activation(out=st[:, 48:64], in_=st[:, 32:48], func=AF.Sqrt, bias=EPS_LN), inc="s")
        b.wait("v", tsd)
        b.emit("v", lambda e: e.reciprocal(out=st[:, 48:64], in_=st[:, 48:64]))
        b.emit("v", lambda e: e.tensor_mul(out=st[:, 0:16], in0=st[:, 32:48], in1=st[:, 48:64]))
        b.emit("v", lambda e: e.tensor_mul(out=st[:, 0:16], in0=st[:, 0:16], in1=st[:, 48:64]))
        tv2 = b.emit("v", lambda e: e.tensor_copy(out=st[:, 32:48], in_=st[:, 48:64]), inc="v")
        b.wait("s", tv2)
        tsd2 = b.emit("s", lambda e: e.activation(out=st[:, 0:16], in_=st[:, 0:16], func=AF.Sqrt, bias=EPS_RMS), inc="s")
        b.wait("v", tsd2)
        b.emit("v", lambda e: e.reciprocal(out=st[:, 0:16], in_=st[:, 0:16]))
        tok_b1c[0] = b.emit("v", lambda e: e.tensor_mul(out=st[:, 0:16], in0=st[:, 0:16], in1=st[:, 32:48]), inc="v")

    b1_tsq = None
    tok_b1c = [None]
    NUMB = {4: SB[0], 5: SB[1], 7: S7}
    numbks = [4, 5, 7]
    for qt in range(NT):
        attn = stg1.ap()[:, (qt % 2) * 1024:(qt % 2) * 1024 + 1024]
        attn_toks = []
        for hg in range(4):
            bk = numbks[(qt * 4 + hg) % 3]
            pv = NUMB[bk]
            b.wait("t", tok_qf)
            b.wait("t", tok_kvS)
            b.wait("t", bank_free[bk])
            tnum = None
            for pj in range(2):
                pair = hg * 2 + pj
                tnum = b.emit("t", lambda e, pp=pair, jj=pj, q0=qt * 128, pvv=pv, st=(pj == 0), sp=(pj == 1): e.matmul(
                    out=pvv[:, jj * 130:(jj + 1) * 130], lhsT=N2q[:, pp, q0:q0 + 128],
                    rhs=kvS[:, pp * 130:(pp + 1) * 130], start=st, stop=sp), inc="t" if pj == 1 else None) or tnum
            tok_num_last = tnum
            if hg == 0 and pend_a4 is not None:
                emit_a4_tail(*pend_a4)
                pend_a4 = None
            b.wait("v", tnum)
            den = pv[:, 0:260].rearrange("p (j c) -> p j c", c=65)[:, :, 64:65]
            st4 = stats.ap()[:, 8:12].rearrange("p (a c) -> p a c", c=1)
            b.emit("v", lambda e, d=den, o=st4: e.tensor_scalar_add(out=o, in0=d, scalar1=EPS_PF))
            b.emit("v", lambda e: e.reciprocal(out=stats.ap()[:, 12:16], in_=stats.ap()[:, 8:12]))
            a0, a1 = broadcast_tensor_aps(
                pv[:, 0:260].rearrange("p (j c) -> p j c", c=65)[:, :, 0:64],
                stats.ap()[:, 12:16].rearrange("p (j c) -> p j c", c=1))
            tok_a = b.emit("v", lambda e, hg_=hg, at=attn, x=a0, y=a1: e.tensor_tensor(
                out=at[:, hg_ * 256:(hg_ + 1) * 256].rearrange("p (j c) -> p j c", c=64),
                in0=x, in1=y, op=ALU.mult), inc="v")
            bank_free[bk] = tok_a
            attn_toks.append(tok_a)
        pend_a4 = (qt, attn, attn_toks)
    emit_a4_tail(*pend_a4)
    tok_r1 = wo_done[-1]

    if upto == "A":
        b.waits("y", wo_done)
        tok_out = dma("y", "dOut", out.rearrange("(t p) c -> p t c", p=128),
                      accT.ap().rearrange("p (t c) -> p t c", c=1024))
        b.wait("y", tok_out)
        return finish(nc, b)

    N2a = N2.ap()
    tok_dkv = dma("g", "dWdkv", N2a[:, 0:4096].rearrange("p (k n) -> p k n", n=512),
                  wd["fWdkv"].rearrange("(k p) n -> p k n", p=128), waits=[tok_num_last])
    tok_q2 = dma("g", "dWq2", N2a[:, 4096:12288].rearrange("p (k n) -> p k n", n=1024),
                 wd["fWq"].rearrange("(k p) n -> p k n", p=128), waits=[tok_num_last])
    tok_uk = dma("g", "dWuk", N2a[:, 12288:16384].rearrange("p (k n) -> p k n", n=1024),
                 wd["fWuk"].rearrange("(k p) n -> p k n", p=128), waits=[tok_num_last])
    b.waits("v", [tok_dkv, tok_g1])
    tok_sdkv = None
    for k in range(KD):
        ap = N2a[:, k * 512:(k + 1) * 512]
        tok_sdkv = b.emit("v", lambda e, a=ap, kk=k: e.tensor_scalar_mul(
            out=a, in0=a, scalar1=g1k.ap()[:, kk:kk + 1]), inc="v" if k == KD - 1 else None) or tok_sdkv
    b.wait("v", tok_q2)
    tok_sq2 = None
    for k in range(KD):
        ap = N2a[:, 4096 + k * 1024:4096 + (k + 1) * 1024]
        tok_sq2 = b.emit("v", lambda e, a=ap, kk=k: e.tensor_scalar_mul(
            out=a, in0=a, scalar1=g1k.ap()[:, kk:kk + 1]), inc="v" if k == KD - 1 else None) or tok_sq2

    N3y = N3.ap().rearrange("p (k n) -> p k n", n=2048)
    st = stats.ap()
    b.wait("v", tok_r1)
    slot_guard = [None, None]
    tok_yn = None
    pend_tr = None
    yn_evs = [None] * NT
    for t in range(NT):
        acc_t = accT.ap()[:, t * 1024:(t + 1) * 1024]
        slot = stg1.ap()[:, (t % 2) * 1024:(t % 2) * 1024 + 1024]
        b.wait("v", slot_guard[t % 2])
        b.wait("v", kf_slot_free[t % 2])
        ty = b.emit("v", lambda e, a=acc_t, sl=slot, tt=t: e.tensor_scalar(
            out=sl, in0=a, scalar1=st[:, 16 + tt:17 + tt], scalar2=st[:, 0 + tt:1 + tt],
            op0=ALU.subtract, op1=ALU.mult), inc="v")
        if pend_tr is not None:
            ev, mm = transpose_tile(*pend_tr[0:3], eng="s")
            slot_guard[pend_tr[3] % 2] = mm
            yn_evs[pend_tr[3]] = ev
        pend_tr = (slot, N3y[:, 0:8, t * 128:(t + 1) * 128], [ty], t)
    ev, mm = transpose_tile(*pend_tr[0:3], eng="s")
    yn_evs[NT - 1] = ev
    tok_yn = ev

    A4c = A4.ap().rearrange("p (l n) -> p l n", n=2048)
    ci = 0
    tok_c = None
    for ml in range(KL):
        for tg in range(4):
            bk = ci % 4
            ci += 1
            b.waits("t", [yn_evs[tg * 4 + 3], tok_sdkv])
            b.wait("t", bank_free[bk])
            tmm = None
            for k in range(KD):
                tmm = b.emit("t", lambda e, kk=k, m=ml, tgg=tg, bb=bk, st=(k == 0), sp=(k == KD - 1): e.matmul(
                    out=QB[bb], lhsT=N2a[:, kk * 512 + m * 128:kk * 512 + (m + 1) * 128],
                    rhs=N3y[:, kk, tgg * 512:(tgg + 1) * 512], start=st, stop=sp),
                    inc="t" if k == KD - 1 else None) or tmm
            eng = "v" if ci % 2 == 0 else "s"
            b.wait(eng, tmm)
            if eng == "v":
                tok_c = b.emit("v", lambda e, bb=bk, m=ml, tgg=tg: e.tensor_copy(
                    out=A4c[:, m, tgg * 512:(tgg + 1) * 512], in_=QB[bb]), inc="v")
            else:
                tok_c = b.emit("s", lambda e, bb=bk, m=ml, tgg=tg: e.activation(
                    out=A4c[:, m, tgg * 512:(tgg + 1) * 512], in_=QB[bb], func=AF.Copy), inc="s")
            bank_free[bk] = tok_c

    tok_qp = None
    last_q_mm = None
    for m in range(KD):
        for tg in range(4):
            bk = ci % 4
            ci += 1
            b.waits("t", [yn_evs[tg * 4 + 3], tok_sq2])
            b.wait("t", bank_free[bk])
            tmm = None
            for k in range(KD):
                tmm = b.emit("t", lambda e, kk=k, mm=m, tgg=tg, bb=bk, st=(k == 0), sp=(k == KD - 1): e.matmul(
                    out=QB[bb], lhsT=N2a[:, 4096 + kk * 1024 + mm * 128:4096 + kk * 1024 + (mm + 1) * 128],
                    rhs=N3y[:, kk, tgg * 512:(tgg + 1) * 512], start=st, stop=sp),
                    inc="t" if k == KD - 1 else None) or tmm
            last_q_mm = tmm
            eng = "v" if ci % 2 == 0 else "s"
            b.wait(eng, tmm)
            if eng == "v":
                tok_qp = b.emit("v", lambda e, bb=bk, mm=m, tgg=tg: e.tensor_copy(
                    out=N1v[:, mm, tgg * 512:(tgg + 1) * 512], in_=QB[bb]), inc="v")
            else:
                tok_qp = b.emit("s", lambda e, bb=bk, mm=m, tgg=tg: e.activation(
                    out=N1v[:, mm, tgg * 512:(tgg + 1) * 512], in_=QB[bb], func=AF.Copy), inc="s")
            bank_free[bk] = tok_qp

    tok_kp = None
    for m in range(KD):
        for tg in range(4):
            bk = ci % 4
            ci += 1
            b.waits("t", [tok_c, tok_uk, last_q_mm])
            b.wait("t", bank_free[bk])
            tmm = None
            for kl in range(KL):
                tmm = b.emit("t", lambda e, kk=kl, mm=m, tgg=tg, bb=bk, st=(kl == 0), sp=(kl == KL - 1): e.matmul(
                    out=QB[bb], lhsT=N2a[:, 12288 + kk * 1024 + mm * 128:12288 + kk * 1024 + (mm + 1) * 128],
                    rhs=A4c[:, kk, tgg * 512:(tgg + 1) * 512], start=st, stop=sp),
                    inc="t" if kl == KL - 1 else None) or tmm
            last_k_mm = tmm
            b.wait("v", tmm)
            tok_kp = b.emit("v", lambda e, bb=bk, mm=m, tgg=tg: e.tensor_copy(
                out=N3y[:, mm, tgg * 512:(tgg + 1) * 512], in_=QB[bb]), inc="v")
            bank_free[bk] = tok_kp

    vON = N2a.rearrange("p (t x) -> p t x", x=1040)
    b.wait("v", last_k_mm)
    tok_von = b.emit("v", lambda e: e.memset(
        vON.rearrange("p t (h c) -> p t h c", c=65)[:, :, :, 64:65], 1.0), inc="v")
    WSr = WSb.ap().rearrange("p (k n) -> p k n", n=1024)
    tok_vp = None
    last_v_mm = None
    for tt in range(NT):
        for half in range(2):
            bk = ci % 4
            ci += 1
            b.waits("t", [tok_c, tok_uv, tok_von, last_k_mm])
            b.wait("t", bank_free[bk])
            tmm = None
            for kl in range(KL):
                tmm = b.emit("t", lambda e, kk=kl, t_=tt, hf=half, bb=bk, st=(kl == 0), sp=(kl == KL - 1): e.matmul(
                    out=QB[bb], lhsT=A4c[:, kk, t_ * 128:(t_ + 1) * 128],
                    rhs=WSr[:, kk, hf * 512:(hf + 1) * 512], start=st, stop=sp),
                    inc="t" if kl == KL - 1 else None) or tmm
            last_v_mm = tmm
            b.wait("v", tmm)
            vdst = vON[:, tt].rearrange("p (h c) -> p h c", c=65)[:, half * 8:(half + 1) * 8, 0:64]
            vsrc = QB[bk].rearrange("p (h c) -> p h c", c=64)
            tok_vp = b.emit("v", lambda e, d=vdst, s=vsrc: e.tensor_copy(out=d, in_=s), inc="v")
            bank_free[bk] = tok_vp

    tok_wo2 = dma("g", "dWo2", A4r, wd["fWo"].rearrange("(k p) n -> p k n", p=128),
                  waits=[last_v_mm])

    QP = [qpad.ap()[:, 0:512], qpad.ap()[:, 512:1024]]

    def v_slice(h, kc):
        return N2a[:, kc * 1040 + h * 65:kc * 1040 + (h + 1) * 65]

    def later(a, c):
        if a is None:
            return c
        if c is None:
            return a
        return a if a[1] >= c[1] else c

    PT = [PTb.ap()[:, 0:1024], PTb.ap()[:, 1024:2048]]
    pt_free = [tok_num_last, tok_num_last]
    tok_qpz = b.emit("v", lambda e: e.memset(qpad.ap(), 0.0), inc="v")
    qp_free = [tok_qpz, tok_qpz]
    MSK = [WSb.ap()[:, t * 512:(t + 1) * 512] for t in range(4)]
    b.wait("g", last_v_mm)
    tok_mm0 = b.emit("g", lambda e: e.memset(WSb.ap()[:, 0:2048], 1.0), inc="g")
    b.wait("g", tok_mm0, force=True)
    tok_msk = None
    for t in range(4):
        tok_msk = b.emit("g", lambda e, tt=t: e.affine_select(
            out=MSK[tt], in_=MSK[tt], pattern=[[1, 512]], compare_op=ALU.is_ge,
            fill=0.0, base=-128 * tt, channel_multiplier=-1), inc="g")
    quad_free = [later(bank_free[0], bank_free[1]), later(bank_free[2], bank_free[3])]
    tail_guard = []
    tok_accg = [None]
    last_tsq = [b1_tsq]
    last_zstt = [None]
    tok_av_final = None
    last_scores = None
    tz = None
    tail_toks = []

    def mslot_ap(qt):
        return (stg1.ap()[:, (qt % 4) * 1024:(qt % 4) * 1024 + 1024] if qt % 4 < 2
                else stg2.ap()[:, (qt % 4 - 2) * 1024:(qt % 4 - 2) * 1024 + 1024])

    def emit_qpad_copy(h, qg):
        par = h % 2
        b.wait("v", qp_free[par])
        b.wait("v", tok_qp)
        return b.emit("v", lambda e, hh=h, g=qg, pr=par: e.tensor_copy(
            out=QP[pr][64 * pr:64 * pr + 64, :],
            in_=N1.ap()[64 * pr:64 * pr + 64,
                        (hh // 2) * 2048 + g * 512:(hh // 2) * 2048 + (g + 1) * 512]), inc="v")

    for qg in range(4):
        nkc = 4 * qg + 4
        nkcg = (nkc + 1) // 2
        av_evac_toks = []
        qc_tok = {}
        if qg == 0:
            qc_tok[0] = emit_qpad_copy(0, qg)
            qc_tok[1] = emit_qpad_copy(1, qg)
        else:
            qc_tok[0], qc_tok[1] = nxt0, nxt1
        for h in range(H):
            par = h % 2
            bkav = 4 + par
            pav = SB[par]
            tqc = qc_tok[h]
            av_first = [True]
            tav = None
            pend = None

            def emit_avs(kcg, tzk, lastgrp):
                nonlocal tav
                b.wait("t", tzk)
                if av_first[0]:
                    b.wait("t", bank_free[bkav])
                nkc_here = min(2, nkc - kcg * 2)
                for j in range(nkc_here):
                    kc = kcg * 2 + j
                    for qj in range(4):
                        qt = qg * 4 + qj
                        if kc > qt:
                            continue
                        last = lastgrp and (kc == nkc - 1 and qj == 3)
                        tav = b.emit("t", lambda e, hh=h, kk=kc, qq=qj, pb=kcg % 2, jj=j, pa=pav,
                                     st=av_first[0], sp=last: e.matmul(
                            out=pa[:, qq * 65:(qq + 1) * 65],
                            lhsT=PT[pb][:, jj * 512 + qq * 128:jj * 512 + (qq + 1) * 128],
                            rhs=v_slice(hh, kk), start=st, stop=sp),
                            inc="t" if last else None) or tav
                        av_first[0] = False
                pt_free[kcg % 2] = tav

            for kcg in range(nkcg):
                qd = kcg % 2
                sv = PSQ.ap()[:, qd * 1024:(qd + 1) * 1024]
                b.waits("t", [tok_qp, tok_kp, tok_vp, tqc])
                b.wait("t", quad_free[qd])
                toks_sc = None
                nkc_here = min(2, nkc - kcg * 2)
                for j in range(nkc_here):
                    kc = kcg * 2 + j
                    toks_sc = b.emit("t", lambda e, hh=h, kk=kc, s=sv, jj=j, pr=par: e.matmul(
                        out=s[:, jj * 512:(jj + 1) * 512],
                        lhsT=N3y[:, hh // 2, kk * 128:(kk + 1) * 128],
                        rhs=QP[pr], start=True, stop=True), inc="t")
                    last_scores = toks_sc
                    qp_free[par] = toks_sc
                if kcg == nkcg - 1 and h + 2 < H:
                    qc_tok[h + 2] = emit_qpad_copy(h + 2, qg)
                ncols = nkc_here * 512
                b.wait("s", toks_sc)
                b.wait("s", pt_free[kcg % 2])
                texp = b.emit("s", lambda e, s=sv, n=ncols, pb=kcg % 2: e.activation(
                    out=PT[pb][:, 0:n], in_=s[:, 0:n], func=AF.Exp, scale=0.125), inc="s")
                quad_free[qd] = texp
                tz = texp
                for j in range(nkc_here):
                    kc = kcg * 2 + j
                    if kc < 4 * qg:
                        continue
                    b.wait("v", texp)
                    b.wait("v", tok_msk)
                    tz = b.emit("v", lambda e, kk=kc, g=qg, pb=kcg % 2, jj=j: e.tensor_mul(
                        out=PT[pb][:, jj * 512:(jj + 1) * 512],
                        in0=PT[pb][:, jj * 512:(jj + 1) * 512], in1=MSK[kk - 4 * g]), inc="v")
                if pend is not None:
                    emit_avs(pend[0], pend[1], False)
                pend = (kcg, tz)
            emit_avs(pend[0], pend[1], True)
            tok_av_final = tav
            b.wait("v", tav)
            den = pav[:, 0:260].rearrange("p (j c) -> p j c", c=65)[:, :, 64:65]
            st4 = stats.ap()[:, 8:12].rearrange("p (a c) -> p a c", c=1)
            b.emit("v", lambda e, d=den, o=st4: e.tensor_copy(out=o, in_=d))
            b.emit("v", lambda e: e.reciprocal(out=stats.ap()[:, 12:16], in_=stats.ap()[:, 8:12]))
            tev = None
            b.waits("v", tail_guard)
            for half2 in range(2):
                stg = stg1 if half2 == 0 else stg2
                a0, a1 = broadcast_tensor_aps(
                    pav[:, half2 * 130:half2 * 130 + 130].rearrange("p (j c) -> p j c", c=65)[:, :, 0:64],
                    stats.ap()[:, 12 + 2 * half2:14 + 2 * half2].rearrange("p (j c) -> p j c", c=1))
                dst = stg.ap()[:, 0:2048].rearrange("p (q c) -> p q c", c=1024)[:, :, h * 64:h * 64 + 64]
                tev = b.emit("v", lambda e, x=a0, y=a1, d=dst: e.tensor_tensor(
                    out=d, in0=x, in1=y, op=ALU.mult), inc="v" if half2 == 1 else None) or tev
            bank_free[bkav] = tev
            av_evac_toks.append(tev)

        if qg < 3:
            nxt0 = emit_qpad_copy(0, qg + 1)
            nxt1 = emit_qpad_copy(1, qg + 1)
        tail_toks = []
        tail_guard = []
        st = stats.ap()
        mla_ev = []
        for qj in range(4):
            ms = mslot_ap(qg * 4 + qj)
            ev, _ = transpose_tile(ms, ms.rearrange("p (k n) -> p k n", n=128), av_evac_toks, eng="s")
            mla_ev.append(ev)
        ztoks = []
        for qj in range(4):
            qt = qg * 4 + qj
            ms = mslot_ap(qt)
            ms_r = ms.rearrange("p (k n) -> p k n", n=128)
            acc_t = accT.ap()[:, qt * 1024:(qt + 1) * 1024]
            y_done = []
            tgc2 = [None, None]
            for half in range(2):
                b.wait("g", last_tsq[0])
                b.wait("g", last_zstt[0])
                if tok_b1c[0] is not None:
                    b.wait("g", tok_b1c[0])
                tgc2[half] = b.emit("g", lambda e, a=acc_t, hf=half, tt=qt: e.tensor_scalar(
                    out=scrF.ap()[:, hf * 512:(hf + 1) * 512], in0=a[:, hf * 512:(hf + 1) * 512],
                    scalar1=stats.ap()[:, 16 + tt:17 + tt], scalar2=stats.ap()[:, 32 + tt:33 + tt],
                    op0=ALU.subtract, op1=ALU.mult), inc="g")
            for half in range(2):
                b.waits("t", [mla_ev[qj], tok_wo2])
                b.wait("t", bank_free[7])
                tw = None
                for k in range(KD):
                    tw = b.emit("t", lambda e, kk=k, hf=half, mr=ms_r, st_=(k == 0), sp=(k == KD - 1): e.matmul(
                        out=S7, lhsT=mr[:, kk, :],
                        rhs=A4r[:, kk, hf * 512:(hf + 1) * 512], start=st_, stop=sp),
                        inc="t" if k == KD - 1 else None) or tw
                b.waits("v", [tw, tgc2[half]])
                ty1 = b.emit("v", lambda e, hf=half: e.tensor_add(
                    out=scrF.ap()[:, hf * 512:(hf + 1) * 512], in0=S7[:, 0:512],
                    in1=scrF.ap()[:, hf * 512:(hf + 1) * 512]), inc="v")
                bank_free[7] = ty1
                y_done.append(ty1)
                b.wait("g", ty1)
                tok_accg[0] = b.emit("g", lambda e, a=acc_t, hf=half: e.tensor_add(
                    out=a[:, hf * 512:(hf + 1) * 512], in0=a[:, hf * 512:(hf + 1) * 512],
                    in1=scrF.ap()[:, hf * 512:(hf + 1) * 512]), inc="g")
            for half in range(2):
                b.waits("s", y_done)
                tsq = b.emit("s", lambda e, hf=half, m_=ms: e.activation(
                    out=m_[:, 0:512], in_=scrF.ap()[:, hf * 512:(hf + 1) * 512], func=AF.Square,
                    accum_out=st[:, hf:hf + 1]), inc="s")
                b.wait("v", tsq)
                last_tsq[0] = tsq
            tms = b.emit("v", lambda e: e.tensor_add(out=st[:, 0:1], in0=st[:, 0:1], in1=st[:, 1:2]), inc="v")
            b.wait("s", tms)
            tsd = b.emit("s", lambda e: e.activation(out=st[:, 2:3], in_=st[:, 0:1], func=AF.Sqrt,
                                                     bias=EPS_RMS, scale=1.0 / 1024), inc="s")
            b.wait("v", tsd)
            b.emit("v", lambda e: e.reciprocal(out=st[:, 2:3], in_=st[:, 2:3]))
            b.wait("v", tok_g2)
            tzt = b.emit("v", lambda e, m_=ms: e.scalar_tensor_tensor(
                out=m_, in0=scrF.ap(), scalar=stats.ap()[:, 2:3], in1=g2b.ap(),
                op0=ALU.mult, op1=ALU.mult), inc="v")
            ztoks.append(tzt)
            last_zstt[0] = tzt
        for qj in range(4):
            qt = qg * 4 + qj
            ms = mslot_ap(qt)
            zev, zmm = transpose_tile(ms, N1v[:, 0:8, qt * 128:(qt + 1) * 128], [ztoks[qj]], eng="s")
            tail_toks.append(zev)
            tail_guard.append(zev)
            bklg = 4 + (qj % 2)
            b.waits("t", [zev, tok_gw])
            b.wait("t", bank_free[bklg])
            tlg = None
            for k in range(KD):
                tlg = b.emit("t", lambda e, kk=k, tt=qt, pb=qj % 2, st_=(k == 0), sp=(k == KD - 1): e.matmul(
                    out=SB[pb][:, 0:8], lhsT=N1v[:, kk, tt * 128:(tt + 1) * 128],
                    rhs=gwb.ap()[:, kk * 8:(kk + 1) * 8], start=st_, stop=sp),
                    inc="t" if k == KD - 1 else None) or tlg
            b.wait("v", tlg)
            tstash = b.emit("v", lambda e, tt=qt, pb=qj % 2: e.tensor_copy(
                out=gts.ap()[:, tt * 8:(tt + 1) * 8], in_=SB[pb][:, 0:8]), inc="v")
            bank_free[bklg] = tstash
            tail_toks.append(tstash)
    tok_z = list(tail_toks)
    tok_mask_last = tz
    bank_free[0] = bank_free[1] = quad_free[0]
    bank_free[2] = bank_free[3] = quad_free[1]

    if upto == "B":
        b.waits("y", tok_z)
        dma("y", "dOut", out.rearrange("(t p) c -> p t c", p=128),
            accT.ap().rearrange("p (t c) -> p t c", c=1024))
        dma("y", "dOut", dbg["zdbg"], N1.ap(), waits=tok_z)
        tok_g3 = dma("y", "dOut", dbg["gdbg"], gts.ap(), waits=tok_z)
        b.wait("y", tok_g3)
        return finish(nc, b)

    W2S = [N3.ap()[:, 0:8192].rearrange("p (k n) -> p k n", n=1024),
           N3.ap()[:, 8192:16384].rearrange("p (k n) -> p k n", n=1024),
           A4.ap().rearrange("p (k n) -> p k n", n=1024)]
    w2r_half = wd["w2"].rearrange("e (hh kh p) n -> e hh p kh n", p=128, kh=8)
    w1slot = [WSb.ap()[:, i * 1024:(i + 1) * 1024] for i in range(4)]
    w1sem = ["dW1a", "dW1b", "dW1c", "dW1d"]
    w1r = wd["w1"].rearrange("e (kc p) (m col) -> e m p kc col", p=128, col=128)
    w2r = wd["w2"].rearrange("e (kh p) n -> e p kh n", p=128)

    panel_consumer = {}
    pidx = 0
    ws_free = tok_msk
    tok_out_last = None

    slot_free = [last_scores, last_scores, tok_qf]
    slot_free[2] = None
    w2sem = ["dW2a", "dW2b", "dW2c"]
    w2tok = [None, None, None]

    hid_free = tok_av_final
    tok_half0 = None
    last_wo2_mm = tw
    slot_free[2] = last_wo2_mm
    mask_last_use = tok_mask_last

    sgc = scrF.ap()
    LG3 = gts.ap().rearrange("p (q c) -> p q c", c=8)
    eqA, eqA3 = sgc[:, 0:128], sgc[:, 0:128].rearrange("p (q c) -> p q c", c=8)
    m2A, m2A3 = sgc[:, 128:256], sgc[:, 128:256].rearrange("p (q c) -> p q c", c=8)
    mx3 = sgc[:, 256:272].rearrange("p (q c) -> p q c", c=1)
    mx23 = sgc[:, 272:288].rearrange("p (q c) -> p q c", c=1)
    ga3 = sgc[:, 304:320].rearrange("p (q c) -> p q c", c=1)
    gb3 = sgc[:, 320:336].rearrange("p (q c) -> p q c", c=1)
    b.wait("v", tok_accg[0])
    b.emit("v", lambda e: e.reduce_max(out=mx3, in_=LG3, axis=AX.X))
    ba0, ba1 = broadcast_tensor_aps(LG3, mx3)
    b.emit("v", lambda e, x=ba0, y=ba1: e.tensor_tensor(out=eqA3, in0=x, in1=y, op=ALU.is_equal))
    b.emit("v", lambda e: e.scalar_tensor_tensor(out=m2A, in0=eqA, scalar=-1e30, in1=gts.ap(),
                                                 op0=ALU.mult, op1=ALU.add))
    b.emit("v", lambda e: e.reduce_max(out=mx23, in_=m2A3, axis=AX.X))
    tmd = b.emit("v", lambda e: e.tensor_tensor(out=sgc[:, 288:304], in0=sgc[:, 256:272],
                                                in1=sgc[:, 272:288], op=ALU.subtract), inc="v")
    b.wait("s", tmd)
    tsg = b.emit("s", lambda e: e.activation(out=sgc[:, 304:320], in_=sgc[:, 288:304],
                                             func=AF.Sigmoid), inc="s")
    b.wait("v", tsg)
    b.emit("v", lambda e: e.tensor_scalar(out=sgc[:, 320:336], in0=sgc[:, 304:320], scalar1=-1.0,
                                          scalar2=1.0, op0=ALU.mult, op1=ALU.add))
    ba2, ba3 = broadcast_tensor_aps(LG3, mx23)
    b.emit("v", lambda e, x=ba2, y=ba3: e.tensor_tensor(out=m2A3, in0=x, in1=y, op=ALU.is_equal))
    ba4, ba5 = broadcast_tensor_aps(m2A3, gb3)
    b.emit("v", lambda e, x=ba4, y=ba5: e.tensor_tensor(out=m2A3, in0=x, in1=y, op=ALU.mult))
    ba6, ba7 = broadcast_tensor_aps(eqA3, ga3)
    b.emit("v", lambda e, x=ba6, y=ba7: e.tensor_tensor(out=eqA3, in0=x, in1=y, op=ALU.mult))
    b.emit("v", lambda e: e.tensor_add(out=gts.ap(), in0=eqA, in1=m2A), inc="v")

    for e in range(NE):
        s_lo, s_hi = (2 * e) % 3, (2 * e + 1) % 3

        def emit_w2(e=e, s_lo=s_lo, s_hi=s_hi):
            for part, sl in ((0, s_lo), (1, s_hi)):
                w = [slot_free[sl]] if slot_free[sl] is not None else []
                w2tok[sl] = dma("g", w2sem[sl], W2S[sl], w2r_half[e, part], waits=w)
        ye_last = None
        for tg in range(2):
            hid_toks = []
            for m in range(KH):
                slot = pidx % 4
                pw = [panel_consumer.get(pidx - 4, mask_last_use or ws_free)]
                ptok = dma("g", w1sem[slot], w1slot[slot].rearrange("p (kc col) -> p kc col", col=128),
                           w1r[e, m], waits=pw)
                if tg == 0 and m == 3:
                    emit_w2()
                bks = ((m % 2) * 2, (m % 2) * 2 + 1)
                pmm = None
                for half in range(2):
                    bk = bks[half]
                    b.wait("t", ptok)
                    b.wait("t", bank_free[bk])
                    b.waits("t", tok_z)
                    for kc in range(KD):
                        pmm = b.emit("t", lambda e_, kk=kc, hf=half, bb=bk, sl=slot, g=tg,
                                     st=(kc == 0), sp=(kc == KD - 1): e_.matmul(
                            out=QB[bb], lhsT=w1slot[sl][:, kk * 128:(kk + 1) * 128],
                            rhs=N1v[:, kk, g * 1024 + hf * 512:g * 1024 + (hf + 1) * 512],
                            start=st, stop=sp),
                            inc="t" if (kc == KD - 1) else None) or pmm
                panel_consumer[pidx] = pmm
                pidx += 1
                b.wait("s", pmm)
                b.wait("s", hid_free)
                tsl = b.emit("s", lambda e_, mm=m: e_.activation(
                    out=N2a[:, mm * 1024:(mm + 1) * 1024], in_=PSQ.ap()[:, (mm % 2) * 1024:(mm % 2) * 1024 + 1024],
                    func=AF.Silu), inc="s")
                bank_free[bks[0]] = bank_free[bks[1]] = tsl
                hid_toks.append(tsl)
            YEB = {4: SB[0], 5: SB[1], 7: S7}
            yeb = [4, 5, 7]
            for tt in range(8):
                qt = tg * 8 + tt
                pair = (yeb[(2 * tt) % 3], yeb[(2 * tt + 1) % 3])
                b.wait("t", bank_free[pair[0]])
                b.wait("t", bank_free[pair[1]])
                b.waits("t", [w2tok[s_lo], w2tok[s_hi]])
                tye = [None, None]
                for kh in range(KH):
                    b.wait("t", hid_toks[kh])
                    ws2 = W2S[s_lo if kh < 8 else s_hi]
                    for half in range(2):
                        tye[half] = b.emit("t", lambda e_, kk=kh, hf=half, t_=tt, pp=pair[half], w2a=ws2,
                                     st=(kh == 0), sp=(kh == KH - 1): e_.matmul(
                            out=YEB[pp][:, 0:512],
                            lhsT=N2a[:, kk * 1024 + t_ * 128:kk * 1024 + t_ * 128 + 128],
                            rhs=w2a[:, kk % 8, hf * 512:(hf + 1) * 512], start=st, stop=sp),
                            inc="t" if (kh == KH - 1) else None) or tye[half]
                ye_last = tye[1]
                for half in range(2):
                    b.wait("v", tye[half])
                    b.wait("v", tok_accg[0])
                    dstp = accT.ap()[:, qt * 1024 + half * 512:qt * 1024 + (half + 1) * 512]
                    tst = b.emit("v", lambda e_, pp=pair[half], d=dstp, tt_=qt, ee=e: e_.scalar_tensor_tensor(
                        out=d, in0=YEB[pp][:, 0:512], scalar=gts.ap()[:, tt_ * 8 + ee:tt_ * 8 + ee + 1],
                        in1=d, op0=ALU.mult, op1=ALU.add), inc="v" if half == 1 else None)
                    bank_free[pair[half]] = tst
                    tok_out_last = tst
                    if e == NE - 1 and tg == 0 and tt == 7 and half == 1:
                        tok_half0 = tst
            hid_free = ye_last
        slot_free[s_lo] = ye_last
        slot_free[s_hi] = ye_last

    b.wait("y", tok_half0)
    dma("y", "dOut", out.rearrange("(t p) c -> p t c", p=128)[:, 0:8, :],
        accT.ap().rearrange("p (t c) -> p t c", c=1024)[:, 0:8, :])
    b.wait("y", tok_out_last)
    tok_out = dma("y", "dOut", out.rearrange("(t p) c -> p t c", p=128)[:, 8:16, :],
                  accT.ap().rearrange("p (t c) -> p t c", c=1024)[:, 8:16, :])
    b.wait("y", tok_out)
    return finish(nc, b)


def finish(nc, b):
    semnames = sorted(b.cnt.keys())
    stack = ExitStack()
    sems = {n: stack.enter_context(nc.semaphore(n)) for n in semnames}
    with nc.Block() as blk:
        blk.tensor(lambda e: b.replay(e, "t", sems))
        blk.vector(lambda e: b.replay(e, "v", sems))
        blk.scalar(lambda e: b.replay(e, "s", sems))
        blk.gpsimd(lambda e: b.replay(e, "g", sems))
        blk.sync(lambda e: b.replay(e, "y", sems))
    stack.close()
    return nc


NAMES = {
    "aWq": "L0_attn_Wq", "aWk": "L0_attn_Wk", "aWv": "L0_attn_Wv", "aWo": "L0_attn_Wo",
    "g1": "L0_ffn_g1", "fWq": "L0_ffn_Wq", "fWdkv": "L0_ffn_Wdkv", "fWuk": "L0_ffn_Wuk",
    "fWuv": "L0_ffn_Wuv", "fWo": "L0_ffn_Wo", "g2": "L0_ffn_g2", "gw": "L0_ffn_gate_w",
    "w1": "L0_ffn_w1", "w2": "L0_ffn_w2",
}


def run(upto, inputs, trace=False):
    nc = build(upto)
    x = np.ascontiguousarray(np.asarray(inputs["L0_attn_x"], dtype=np.float32))
    wmap = {k: np.ascontiguousarray(np.asarray(inputs[v], dtype=np.float32)) for k, v in NAMES.items()}
    in_maps = [dict(wmap, x=x[i]) for i in range(8)]
    return run_bass_kernel_spmd(nc, in_maps, core_ids=list(range(8)), trace=trace)


def kernel(**inputs):
    res = run("full", inputs)
    return np.stack([res.results[i]["out"] for i in range(8)], axis=0)


# version 32
# speedup vs baseline: 1.0014x; 1.0014x over previous
import numpy as np
from contextlib import ExitStack
import concourse.bass as bass
import concourse.mybir as mybir
from concourse.bass import broadcast_tensor_aps
from concourse.bass_utils import run_bass_kernel_spmd

F32 = mybir.dt.float32
BF16 = mybir.dt.bfloat16
AF = mybir.ActivationFunctionType
ALU = mybir.AluOpType
AX = mybir.AxisListType

S, D, H, DH, LAT, NE, ED = 2048, 1024, 16, 64, 512, 8, 2048
NT = S // 128
KD = D // 128
KL = LAT // 128
KH = ED // 128
EPS_LN, EPS_PF, EPS_RMS = 1e-5, 1e-6, 1e-6

ENGS = ("t", "v", "s", "g", "y")


class Bld:

    def __init__(self):
        self.ops = {e: [] for e in ENGS}
        self.cnt = {e: 0 for e in ENGS}
        self.waited = {e: {} for e in ENGS}

    AUTOCHAIN = ("v", "s", "g")

    def emit(self, e, maker, inc=None, n=1):
        if e in self.AUTOCHAIN and inc is None:
            inc, n = e, 1
        if e in self.AUTOCHAIN and inc == e:
            prev = self.cnt.get(e, 0)
            if prev > 0:
                self.wait(e, (e, prev), force=True)
        tok = None
        if inc is not None:
            self.cnt[inc] = self.cnt.get(inc, 0) + n
            tok = (inc, self.cnt[inc])
        self.ops[e].append((maker, inc, n))
        return tok

    def wait(self, e, tok, force=False):
        if tok is None:
            return
        sem, val = tok
        if sem == e and not force:
            return
        if self.waited[e].get(sem, 0) >= val:
            return
        self.waited[e][sem] = val
        self.ops[e].append((("w", sem, val), None, 0))

    def waits(self, e, toks):
        for t in toks:
            self.wait(e, t)

    def replay(self, eng, name, sems):
        for maker, inc, n in self.ops[name]:
            if isinstance(maker, tuple):
                eng.wait_ge(sems[maker[1]], maker[2])
            else:
                ins = maker(eng)
                if inc is not None:
                    ins.then_inc(sems[inc], n)


def build(upto="full"):
    nc = bass.Bass("TRN2", target_bir_lowering=False)

    xin = nc.dram_tensor("x", [S, D], F32, kind="ExternalInput").ap()
    out = nc.dram_tensor("out", [S, D], F32, kind="ExternalOutput").ap()
    wd = {}
    for name, shape in [
        ("aWq", [D, D]), ("aWk", [D, D]), ("aWv", [D, D]), ("aWo", [D, D]),
        ("g1", [D]), ("fWq", [D, D]), ("fWdkv", [D, LAT]), ("fWuk", [LAT, D]),
        ("fWuv", [LAT, D]), ("fWo", [D, D]), ("g2", [D]), ("gw", [D, NE]),
        ("w1", [NE, D, ED]), ("w2", [NE, ED, D]),
    ]:
        wd[name] = nc.dram_tensor(name, shape, F32, kind="ExternalInput").ap()

    dbg = {}
    if upto == "B":
        dbg["zdbg"] = nc.dram_tensor("zdbg", [128, 16384], BF16, kind="ExternalOutput").ap()
        dbg["gdbg"] = nc.dram_tensor("gdbg", [128, 128], F32, kind="ExternalOutput").ap()

    accT = nc.alloc_sbuf_tensor("accT", [128, NT * 1024], F32)
    N1 = nc.alloc_sbuf_tensor("N1", [128, 16384], BF16)
    N2 = nc.alloc_sbuf_tensor("N2", [128, 16640], BF16)
    N3 = nc.alloc_sbuf_tensor("N3", [128, 16384], BF16)
    A4 = nc.alloc_sbuf_tensor("A4", [128, 8192], BF16)
    WSb = nc.alloc_sbuf_tensor("WSb", [128, 4096], BF16)
    stg1 = nc.alloc_sbuf_tensor("stg1", [128, 2048], BF16)
    stg2 = nc.alloc_sbuf_tensor("stg2", [128, 2080], BF16)
    PTb = nc.alloc_sbuf_tensor("PTb", [128, 2048], BF16)
    g2b = nc.alloc_sbuf_tensor("g2b", [128, 1024], BF16)
    g1k = nc.alloc_sbuf_tensor("g1k", [128, 8], F32)
    gwb = nc.alloc_sbuf_tensor("gwb", [128, 64], BF16)
    idn = nc.alloc_sbuf_tensor("idn", [128, 128], BF16)
    scrF = nc.alloc_sbuf_tensor("scrF", [128, 1024], F32)
    qpad = nc.alloc_sbuf_tensor("qpad", [128, 1024], BF16)
    stats = nc.alloc_sbuf_tensor("stats", [128, 72], F32)
    gts = nc.alloc_sbuf_tensor("gts", [128, 128], F32)
    cst = nc.alloc_sbuf_tensor("cst", [128, 8], F32)

    CONSTS = [EPS_LN, EPS_RMS, 0.0, 1.0, -1.0, 1.0 / 1024, -1e30, EPS_PF]
    for i, v in enumerate(CONSTS):
        nc.const_aps.register(F32, v, cst.ap()[:, i:i + 1])

    PSQ = nc.alloc_psum_tensor("PSQ", [128, 2048], F32)
    PS4 = nc.alloc_psum_tensor("PS4", [128, 512], F32)
    PS5 = nc.alloc_psum_tensor("PS5", [128, 512], F32)
    PST = nc.alloc_psum_tensor("PST", [128, 1024], BF16)
    PS7 = nc.alloc_psum_tensor("PS7", [128, 512], F32)
    QB = [PSQ.ap()[:, i * 512:(i + 1) * 512] for i in range(4)]
    SB = [PS4.ap(), PS5.ap()]
    S7 = PS7.ap()
    PSTa = PST.ap()
    PSTr = PST.ap().rearrange("p (k n) -> p k n", n=128)

    b = Bld()
    bank_free = [None] * 8

    def mk_memsets(aps_vals):
        def f(e):
            ins = None
            for a, v in aps_vals:
                ins = e.memset(a, v)
            return ins
        return f

    tok_cst = b.emit("v", mk_memsets([(cst.ap()[:, i:i + 1], v) for i, v in enumerate(CONSTS)]), inc="v")
    b.wait("s", tok_cst)
    tok_ms = b.emit("g", lambda e: e.memset(idn.ap(), 1.0), inc="g")
    b.wait("g", tok_ms, force=True)
    tok_idn = b.emit("g", lambda e: e.affine_select(
        out=idn.ap(), in_=idn.ap(), pattern=[[1, 128]],
        compare_op=ALU.is_equal, fill=0.0, base=0, channel_multiplier=-1), inc="g")
    tok_vones = []
    for half in range(2):
        ap = stg2.ap()[:, half * 1040:half * 1040 + 1040].rearrange(
            "p (h c) -> p h c", c=65)[:, :, 64:65]
        tok_vones.append(b.emit("v", lambda e, a=ap: e.memset(a, 1.0), inc="v"))

    def dma(eng, sem, dst, src, waits=(), slow=False):
        b.waits(eng, waits)
        return b.emit(eng, lambda e, d=dst, s=src, sl=slow: e.dma_start(
            out=d, in_=s, allow_slow_non_contiguous=sl), inc=sem, n=16)

    tok_xc = []
    for xc in range(4):
        tok_xc.append(dma("y", "dX%d" % xc,
                      accT.ap().rearrange("p (t c) -> p t c", c=1024)[:, xc * 4:(xc + 1) * 4, :],
                      xin.rearrange("(t p) c -> p t c", p=128)[:, xc * 4:(xc + 1) * 4, :]))
    tok_x = tok_xc[-1]

    N3r = N3.ap().rearrange("p (k n) -> p k n", n=1024)
    A4r = A4.ap()[:, 0:8192].rearrange("p (k n) -> p k n", n=1024)
    tok_wk = dma("g", "dWk", N3r[:, 0:8, :], wd["aWk"].rearrange("(k p) n -> p k n", p=128))
    tok_wv = dma("g", "dWv", N3r[:, 8:16, :], wd["aWv"].rearrange("(k p) n -> p k n", p=128))
    tok_wq = dma("g", "dWq", A4r, wd["aWq"].rearrange("(k p) n -> p k n", p=128))
    tok_g1 = dma("g", "dG1", g1k.ap(), wd["g1"].rearrange("(k p) -> p k", p=128), slow=True)
    tok_g2 = dma("g", "dG2", g2b.ap(),
                 wd["g2"].rearrange("(a n) -> a n", a=1).partition_broadcast(128))
    tok_gw = dma("g", "dGw", gwb.ap().rearrange("p (k n) -> p k n", n=8),
                 wd["gw"].rearrange("(k p) n -> p k n", p=128))
    tok_uv = dma("g", "dWuv", WSb.ap().rearrange("p (k n) -> p k n", n=1024),
                 wd["fWuv"].rearrange("(k p) n -> p k n", p=128))

    ln_guard = [None]

    def ln_stats(src_ap, wait_t=None):
        st = stats.ap()
        b.emit("v", lambda e: e.reduce_sum(out=st[:, 0:1], in_=src_ap, axis=AX.X))
        b.emit("v", lambda e: e.tensor_scalar_mul(out=st[:, 1:2], in0=st[:, 0:1], scalar1=1.0 / 1024))
        b.wait("s", ln_guard[0])
        if wait_t is not None:
            b.wait("s", wait_t)
        tsq = b.emit("s", lambda e: e.activation(out=scrF.ap(), in_=src_ap, func=AF.Square,
                                                 accum_out=st[:, 2:3]), inc="s")
        b.wait("v", tsq)
        b.emit("v", lambda e: e.tensor_scalar_mul(out=st[:, 2:3], in0=st[:, 2:3], scalar1=1.0 / 1024))
        b.emit("v", lambda e: e.tensor_mul(out=st[:, 3:4], in0=st[:, 1:2], in1=st[:, 1:2]))
        tv = b.emit("v", lambda e: e.tensor_sub(out=st[:, 3:4], in0=st[:, 2:3], in1=st[:, 3:4]), inc="v")
        b.wait("s", tv)
        b.wait("s", tok_cst)
        tsd = b.emit("s", lambda e: e.activation(out=st[:, 4:5], in_=st[:, 3:4], func=AF.Sqrt, bias=EPS_LN), inc="s")
        b.wait("v", tsd)
        trs = b.emit("v", lambda e: e.reciprocal(out=st[:, 5:6], in_=st[:, 4:5]), inc="v")
        ln_guard[0] = trs
        return trs

    def transpose_tile(src16, dst_ap, waits, eng="v"):
        b.waits("t", waits)
        b.wait("t", bank_free[6])
        tok = None
        for c in range(8):
            tok = b.emit("t", lambda e, cc=c: e.matmul(
                out=PSTa[:, cc * 128:(cc + 1) * 128], lhsT=src16[:, cc * 128:(cc + 1) * 128],
                rhs=idn.ap(), is_transpose=True, start=(cc == 0), stop=(cc == 7)),
                inc="t" if c == 7 else None)
        b.wait(eng, tok)
        if eng == "v":
            ev = b.emit("v", lambda e, d=dst_ap: e.tensor_copy(out=d, in_=PSTr), inc="v")
        else:
            ev = b.emit("s", lambda e, d=dst_ap: e.activation(out=d, in_=PSTr, func=AF.Copy), inc="s")
        bank_free[6] = ev
        return ev, tok

    N1v = N1.ap().rearrange("p (k n) -> p k n", n=2048)

    b.wait("t", tok_idn)
    scr_guard = [None, None]
    slot_guard = [None, None]
    tok_aFM = None
    st = stats.ap()
    b.wait("s", tok_cst)
    pend_tr = None
    for c in range(4):
        lo = 4 * c
        b.wait("v", tok_xc[c])
        b.wait("s", tok_xc[c])
        tsq = None
        for t in range(lo, lo + 4):
            acc_t = accT.ap()[:, t * 1024:(t + 1) * 1024]
            b.emit("s", lambda e, a=acc_t, tt=t: e.activation(
                out=scrF.ap(), in_=a, func=AF.Copy, accum_out=st[:, 16 + tt:17 + tt]))
            tsq = b.emit("s", lambda e, a=acc_t, tt=t: e.activation(
                out=scrF.ap(), in_=a, func=AF.Square, accum_out=st[:, 32 + tt:33 + tt]), inc="s")
        b.wait("v", tsq)
        b.emit("v", lambda e, l=lo: e.tensor_scalar_mul(out=st[:, 16 + l:20 + l], in0=st[:, 16 + l:20 + l], scalar1=1.0 / 1024))
        b.emit("v", lambda e, l=lo: e.tensor_scalar_mul(out=st[:, 32 + l:36 + l], in0=st[:, 32 + l:36 + l], scalar1=1.0 / 1024))
        b.emit("v", lambda e, l=lo: e.tensor_mul(out=st[:, l:l + 4], in0=st[:, 16 + l:20 + l], in1=st[:, 16 + l:20 + l]))
        tv = b.emit("v", lambda e, l=lo: e.tensor_sub(out=st[:, 32 + l:36 + l], in0=st[:, 32 + l:36 + l], in1=st[:, l:l + 4]), inc="v")
        b.wait("s", tv)
        tsd = b.emit("s", lambda e, l=lo: e.activation(out=st[:, 32 + l:36 + l], in_=st[:, 32 + l:36 + l], func=AF.Sqrt, bias=EPS_LN), inc="s")
        b.wait("v", tsd)
        b.emit("v", lambda e, l=lo: e.reciprocal(out=st[:, 32 + l:36 + l], in_=st[:, 32 + l:36 + l]))
        for t in range(lo, lo + 4):
            acc_t = accT.ap()[:, t * 1024:(t + 1) * 1024]
            slot = stg1.ap()[:, (t % 2) * 1024:(t % 2) * 1024 + 1024]
            b.wait("v", slot_guard[t % 2])
            ta = b.emit("v", lambda e, a=acc_t, sl=slot, tt=t: e.tensor_scalar(
                out=sl, in0=a, scalar1=st[:, 16 + tt:17 + tt], scalar2=st[:, 32 + tt:33 + tt],
                op0=ALU.subtract, op1=ALU.mult), inc="v")
            if pend_tr is not None:
                ev, mm = transpose_tile(*pend_tr[0:3])
                slot_guard[pend_tr[3] % 2] = mm
            pend_tr = (slot, N1v[:, 0:8, t * 128:(t + 1) * 128], [ta], t)
    ev, mm = transpose_tile(*pend_tr[0:3])
    tok_aFM = ev

    KVB = [SB[0], SB[1], S7]

    def kv_view(h):
        return KVB[h // 7][0:64, 65 * (h % 7):65 * (h % 7) + 65]

    kf_slot_free = [slot_guard[0], slot_guard[1]]
    v_slot_free = [tok_vones[0], tok_vones[1]]
    kv_last = None
    last_kproj_mm = None
    pend_kv = None

    def emit_kv(tt, kfa, vpa, tks):
        nonlocal kv_last
        b.waits("t", tks)
        for h in range(H):
            first = (tt == 0 and h % 7 == 0)
            last = (tt == NT - 1 and h in (6, 13, 15))
            kv_last = b.emit("t", lambda e, hh=h, kf_=kfa, vp_=vpa, st_=first, sp=last: e.matmul(
                out=kv_view(hh), lhsT=kf_[:, hh * 64:(hh + 1) * 64],
                rhs=vp_[:, hh * 65:(hh + 1) * 65], start=st_, stop=sp),
                inc="t" if h == H - 1 else None)
        kf_slot_free[tt % 2] = kv_last
        v_slot_free[tt % 2] = kv_last

    for tt in range(NT):
        b.waits("t", [tok_aFM, tok_wk, tok_wv])
        for bk in range(4):
            b.wait("t", bank_free[bk])
        tok_kv = None
        for k in range(KD):
            aT = N1v[:, k, tt * 128:(tt + 1) * 128]
            for j, (rhs, bk) in enumerate([
                (N3r[:, k, 0:512], 0), (N3r[:, k, 512:1024], 1),
                (N3r[:, 8 + k, 0:512], 2), (N3r[:, 8 + k, 512:1024], 3),
            ]):
                inc = "t" if (k == KD - 1 and j == 3) else None
                tok_kv = b.emit("t", lambda e, a=aT, r=rhs, bb=bk, st=(k == 0), sp=(k == KD - 1): e.matmul(
                    out=QB[bb], lhsT=a, rhs=r, start=st, stop=sp), inc=inc) or tok_kv
        last_kproj_mm = tok_kv
        if pend_kv is not None:
            emit_kv(*pend_kv)
            pend_kv = None
        sl = tt % 2
        kf = stg1.ap()[:, sl * 1024:sl * 1024 + 1024]
        vp = stg2.ap()[:, sl * 1040:sl * 1040 + 1040]
        b.wait("v", tok_kv)
        b.wait("v", kf_slot_free[sl])
        kpsum = PSQ.ap()[:, 0:1024]
        b.wait("v", scr_guard[0])
        b.emit("v", lambda e, kp=kpsum, d=kf: e.tensor_scalar_max(out=d[:, 0:512], in0=kp[:, 0:512], scalar1=0.0))
        tmin0 = b.emit("v", lambda e, kp=kpsum: e.tensor_scalar_min(out=scrF.ap()[:, 0:512], in0=kp[:, 0:512], scalar1=0.0), inc="v")
        b.emit("v", lambda e, kp=kpsum, d=kf: e.tensor_scalar_max(out=d[:, 512:1024], in0=kp[:, 512:1024], scalar1=0.0))
        tmin = b.emit("v", lambda e, kp=kpsum: e.tensor_scalar_min(out=scrF.ap()[:, 512:1024], in0=kp[:, 512:1024], scalar1=0.0), inc="v")
        vdst = vp.rearrange("p (h c) -> p h c", c=65)[:, :, 0:64]
        vsrc = PSQ.ap()[:, 1024:2048].rearrange("p (h c) -> p h c", c=64)
        b.wait("s", tok_kv)
        b.wait("s", v_slot_free[sl])
        tv = b.emit("s", lambda e, d=vdst, s_=vsrc: e.activation(out=d, in_=s_, func=AF.Copy), inc="s")
        b.wait("s", tmin)
        b.wait("s", scr_guard[0])
        te = b.emit("s", lambda e: e.activation(out=scrF.ap(), in_=scrF.ap(), func=AF.Exp), inc="s")
        bank_free[0] = tmin0
        bank_free[1] = tmin
        bank_free[2] = bank_free[3] = tv
        b.wait("g", te)
        tk = b.emit("g", lambda e, d=kf: e.tensor_add(out=d, in0=d, in1=scrF.ap()), inc="g")
        scr_guard[0] = tk
        pend_kv = (tt, kf, vp, [tk, tv])
    emit_kv(*pend_kv)
    kvS = PTb.ap()
    kstg = stg1.ap()[0:64, 0:1040]
    b.wait("v", kv_last)
    b.emit("v", lambda e: e.tensor_copy(out=kstg[:, 0:455], in_=SB[0][0:64, 0:455]))
    b.emit("v", lambda e: e.tensor_copy(out=kstg[:, 455:910], in_=SB[1][0:64, 0:455]))
    b.emit("v", lambda e: e.tensor_copy(out=kstg[:, 910:1040], in_=S7[0:64, 0:130]))
    tok_kvSv = b.emit("v", lambda e: e.memset(kvS[:, 0:1040], 0.0), inc="v")
    bank_free[4] = bank_free[5] = bank_free[7] = tok_kvSv
    bd = lambda a: a.rearrange("p (j c) -> p j c", c=130)
    dma("y", "dKV", bd(kvS[0:64, 0:1040])[:, :, 0:65],
        bd(kstg)[:, :, 0:65], waits=[tok_kvSv])
    tok_kvS = dma("y", "dKV", bd(kvS[64:128, 0:1040])[:, :, 65:130],
                  bd(kstg)[:, :, 65:130])

    if upto == "A2":
        b.waits("y", [tok_kvS])
        tok_out = dma("y", "dOut", out.rearrange("(t p) c -> p t c", p=128),
                      accT.ap().rearrange("p (t c) -> p t c", c=1024))
        b.wait("y", tok_out)
        return finish(nc, b)

    tok_wo = dma("g", "dWo", N3r[:, 0:8, :], wd["aWo"].rearrange("(k p) n -> p k n", p=128),
                 waits=[last_kproj_mm])

    N2q = N2.ap()[:, 0:16384].rearrange("p (m n) -> p m n", n=2048)
    tok_qf = None
    last_qproj_mm = None
    qi = 0
    for m in range(KD):
        for tg in range(4):
            bk = qi % 4
            qi += 1
            b.waits("t", [tok_wq, tok_aFM])
            b.wait("t", bank_free[bk])
            tmm = None
            for k in range(KD):
                tmm = b.emit("t", lambda e, kk=k, mm=m, tgg=tg, bb=bk, st=(k == 0), sp=(k == KD - 1): e.matmul(
                    out=QB[bb], lhsT=A4r[:, kk, mm * 128:(mm + 1) * 128],
                    rhs=N1v[:, kk, tgg * 512:(tgg + 1) * 512], start=st, stop=sp),
                    inc="t" if k == KD - 1 else None) or tmm
            last_qproj_mm = tmm
            b.wait("v", tmm)
            sc = scrF.ap()[:, (qi % 2) * 512:(qi % 2) * 512 + 512]
            dst = N2q[:, m, tg * 512:(tg + 1) * 512]
            b.emit("v", lambda e, bb=bk, d=dst: e.tensor_scalar_max(out=d, in0=QB[bb], scalar1=0.0))
            b.wait("v", scr_guard[qi % 2])
            tmin = b.emit("v", lambda e, bb=bk, s=sc: e.tensor_scalar_min(out=s, in0=QB[bb], scalar1=0.0), inc="v")
            b.wait("s", tmin)
            b.wait("s", scr_guard[qi % 2])
            te = b.emit("s", lambda e, s=sc: e.activation(out=s, in_=s, func=AF.Exp), inc="s")
            b.wait("g", te)
            tok_qf = b.emit("g", lambda e, d=dst, s=sc: e.tensor_add(out=d, in0=d, in1=s), inc="g")
            bank_free[bk] = tmin
            scr_guard[qi % 2] = tok_qf

    if upto == "A3":
        b.waits("y", [tok_qf])
        tok_out = dma("y", "dOut", out.rearrange("(t p) c -> p t c", p=128),
                      accT.ap().rearrange("p (t c) -> p t c", c=1024))
        b.wait("y", tok_out)
        return finish(nc, b)



    wo_done = []
    tok_num_last = None
    pend_a4 = None
    prev_scale_tok = [None]

    def emit_a4_tail(qt, attn, attn_toks):
        nonlocal wo_done, b1_tsq
        afm = stg2.ap()[:, 0:1024]
        afm_r = afm.rearrange("p (k n) -> p k n", n=128)
        ev, _ = transpose_tile(attn, afm_r, attn_toks, eng="s")
        evs = [ev]
        for half in range(2):
            bk = (qt % 2) * 2 + half
            b.waits("t", evs + [tok_wo])
            b.wait("t", bank_free[bk])
            tw = None
            for k in range(KD):
                tw = b.emit("t", lambda e, kk=k, hf=half, bb=bk, ar=afm_r, st=(k == 0), sp=(k == KD - 1): e.matmul(
                    out=QB[bb], lhsT=ar[:, kk, :], rhs=N3r[:, kk, hf * 512:(hf + 1) * 512],
                    start=st, stop=sp), inc="t" if k == KD - 1 else None) or tw
            b.wait("v", tw)
            dstp = accT.ap()[:, qt * 1024 + half * 512:qt * 1024 + (half + 1) * 512]
            tr1 = b.emit("v", lambda e, bb=bk, d=dstp: e.tensor_add(out=d, in0=QB[bb], in1=d), inc="v")
            bank_free[bk] = tr1
            wo_done.append(tr1)
        a_t = accT.ap()[:, qt * 1024:(qt + 1) * 1024]
        stp = stats.ap()
        b.emit("v", lambda e, a=a_t, tt=qt: e.reduce_sum(out=stp[:, 16 + tt:17 + tt], in_=a, axis=AX.X))
        b.wait("s", scr_guard[0])
        b.wait("s", scr_guard[1])
        b.wait("s", tr1)
        b1_tsq = b.emit("s", lambda e, a=a_t, tt=qt: e.activation(
            out=scrF.ap(), in_=a, func=AF.Square, accum_out=stp[:, 32 + tt:33 + tt]), inc="s")
        if qt == NT - 1:
            emit_b1_chain()

    def emit_b1_chain():
        st = stats.ap()
        b.wait("v", b1_tsq)
        b.emit("v", lambda e: e.tensor_scalar_mul(out=st[:, 16:32], in0=st[:, 16:32], scalar1=1.0 / 1024))
        b.emit("v", lambda e: e.tensor_scalar_mul(out=st[:, 32:48], in0=st[:, 32:48], scalar1=1.0 / 1024))
        b.emit("v", lambda e: e.tensor_mul(out=st[:, 0:16], in0=st[:, 16:32], in1=st[:, 16:32]))
        tv = b.emit("v", lambda e: e.tensor_sub(out=st[:, 32:48], in0=st[:, 32:48], in1=st[:, 0:16]), inc="v")
        b.wait("s", tv)
        tsd = b.emit("s", lambda e: e.activation(out=st[:, 48:64], in_=st[:, 32:48], func=AF.Sqrt, bias=EPS_LN), inc="s")
        b.wait("v", tsd)
        b.emit("v", lambda e: e.reciprocal(out=st[:, 48:64], in_=st[:, 48:64]))
        b.emit("v", lambda e: e.tensor_mul(out=st[:, 0:16], in0=st[:, 32:48], in1=st[:, 48:64]))
        b.emit("v", lambda e: e.tensor_mul(out=st[:, 0:16], in0=st[:, 0:16], in1=st[:, 48:64]))
        tv2 = b.emit("v", lambda e: e.tensor_copy(out=st[:, 32:48], in_=st[:, 48:64]), inc="v")
        b.wait("s", tv2)
        tsd2 = b.emit("s", lambda e: e.activation(out=st[:, 0:16], in_=st[:, 0:16], func=AF.Sqrt, bias=EPS_RMS), inc="s")
        b.wait("v", tsd2)
        b.emit("v", lambda e: e.reciprocal(out=st[:, 0:16], in_=st[:, 0:16]))
        tok_b1c[0] = b.emit("v", lambda e: e.tensor_mul(out=st[:, 0:16], in0=st[:, 0:16], in1=st[:, 32:48]), inc="v")

    b1_tsq = None
    tok_b1c = [None]
    NUMB = {4: SB[0], 5: SB[1], 7: S7}
    numbks = [4, 5, 7]
    for qt in range(NT):
        attn = stg1.ap()[:, (qt % 2) * 1024:(qt % 2) * 1024 + 1024]
        attn_toks = []
        for hg in range(4):
            bk = numbks[(qt * 4 + hg) % 3]
            pv = NUMB[bk]
            b.wait("t", tok_qf)
            b.wait("t", tok_kvS)
            b.wait("t", bank_free[bk])
            tnum = None
            for pj in range(2):
                pair = hg * 2 + pj
                tnum = b.emit("t", lambda e, pp=pair, jj=pj, q0=qt * 128, pvv=pv, st=(pj == 0), sp=(pj == 1): e.matmul(
                    out=pvv[:, jj * 130:(jj + 1) * 130], lhsT=N2q[:, pp, q0:q0 + 128],
                    rhs=kvS[:, pp * 130:(pp + 1) * 130], start=st, stop=sp), inc="t" if pj == 1 else None) or tnum
            tok_num_last = tnum
            if hg == 0 and pend_a4 is not None:
                emit_a4_tail(*pend_a4)
                pend_a4 = None
            b.wait("v", tnum)
            den = pv[:, 0:260].rearrange("p (j c) -> p j c", c=65)[:, :, 64:65]
            st4 = stats.ap()[:, 8:12].rearrange("p (a c) -> p a c", c=1)
            b.emit("v", lambda e, d=den, o=st4: e.tensor_scalar_add(out=o, in0=d, scalar1=EPS_PF))
            b.emit("v", lambda e: e.reciprocal(out=stats.ap()[:, 12:16], in_=stats.ap()[:, 8:12]))
            a0, a1 = broadcast_tensor_aps(
                pv[:, 0:260].rearrange("p (j c) -> p j c", c=65)[:, :, 0:64],
                stats.ap()[:, 12:16].rearrange("p (j c) -> p j c", c=1))
            tok_a = b.emit("v", lambda e, hg_=hg, at=attn, x=a0, y=a1: e.tensor_tensor(
                out=at[:, hg_ * 256:(hg_ + 1) * 256].rearrange("p (j c) -> p j c", c=64),
                in0=x, in1=y, op=ALU.mult), inc="v")
            bank_free[bk] = tok_a
            attn_toks.append(tok_a)
        pend_a4 = (qt, attn, attn_toks)
    emit_a4_tail(*pend_a4)
    tok_r1 = wo_done[-1]

    if upto == "A":
        b.waits("y", wo_done)
        tok_out = dma("y", "dOut", out.rearrange("(t p) c -> p t c", p=128),
                      accT.ap().rearrange("p (t c) -> p t c", c=1024))
        b.wait("y", tok_out)
        return finish(nc, b)

    N2a = N2.ap()
    tok_dkv = dma("g", "dWdkv", N2a[:, 0:4096].rearrange("p (k n) -> p k n", n=512),
                  wd["fWdkv"].rearrange("(k p) n -> p k n", p=128), waits=[tok_num_last])
    tok_q2 = dma("g", "dWq2", N2a[:, 4096:12288].rearrange("p (k n) -> p k n", n=1024),
                 wd["fWq"].rearrange("(k p) n -> p k n", p=128), waits=[tok_num_last])
    tok_uk = dma("g", "dWuk", N2a[:, 12288:16384].rearrange("p (k n) -> p k n", n=1024),
                 wd["fWuk"].rearrange("(k p) n -> p k n", p=128), waits=[tok_num_last])
    b.waits("v", [tok_dkv, tok_g1])
    tok_sdkv = None
    for k in range(KD):
        ap = N2a[:, k * 512:(k + 1) * 512]
        tok_sdkv = b.emit("v", lambda e, a=ap, kk=k: e.tensor_scalar_mul(
            out=a, in0=a, scalar1=g1k.ap()[:, kk:kk + 1]), inc="v" if k == KD - 1 else None) or tok_sdkv
    b.wait("v", tok_q2)
    tok_sq2 = None
    for k in range(KD):
        ap = N2a[:, 4096 + k * 1024:4096 + (k + 1) * 1024]
        tok_sq2 = b.emit("v", lambda e, a=ap, kk=k: e.tensor_scalar_mul(
            out=a, in0=a, scalar1=g1k.ap()[:, kk:kk + 1]), inc="v" if k == KD - 1 else None) or tok_sq2

    N3y = N3.ap().rearrange("p (k n) -> p k n", n=2048)
    st = stats.ap()
    b.wait("v", tok_r1)
    slot_guard = [None, None]
    tok_yn = None
    pend_tr = None
    yn_evs = [None] * NT
    for t in range(NT):
        acc_t = accT.ap()[:, t * 1024:(t + 1) * 1024]
        slot = stg1.ap()[:, (t % 2) * 1024:(t % 2) * 1024 + 1024]
        b.wait("v", slot_guard[t % 2])
        b.wait("v", kf_slot_free[t % 2])
        ty = b.emit("v", lambda e, a=acc_t, sl=slot, tt=t: e.tensor_scalar(
            out=sl, in0=a, scalar1=st[:, 16 + tt:17 + tt], scalar2=st[:, 0 + tt:1 + tt],
            op0=ALU.subtract, op1=ALU.mult), inc="v")
        if pend_tr is not None:
            ev, mm = transpose_tile(*pend_tr[0:3])
            slot_guard[pend_tr[3] % 2] = mm
            yn_evs[pend_tr[3]] = ev
        pend_tr = (slot, N3y[:, 0:8, t * 128:(t + 1) * 128], [ty], t)
    ev, mm = transpose_tile(*pend_tr[0:3])
    yn_evs[NT - 1] = ev
    tok_yn = ev

    A4c = A4.ap().rearrange("p (l n) -> p l n", n=2048)
    ci = 0
    tok_c = None
    for ml in range(KL):
        for tg in range(4):
            bk = ci % 4
            ci += 1
            b.waits("t", [yn_evs[tg * 4 + 3], tok_sdkv])
            b.wait("t", bank_free[bk])
            tmm = None
            for k in range(KD):
                tmm = b.emit("t", lambda e, kk=k, m=ml, tgg=tg, bb=bk, st=(k == 0), sp=(k == KD - 1): e.matmul(
                    out=QB[bb], lhsT=N2a[:, kk * 512 + m * 128:kk * 512 + (m + 1) * 128],
                    rhs=N3y[:, kk, tgg * 512:(tgg + 1) * 512], start=st, stop=sp),
                    inc="t" if k == KD - 1 else None) or tmm
            eng = "v" if ci % 2 == 0 else "s"
            b.wait(eng, tmm)
            if eng == "v":
                tok_c = b.emit("v", lambda e, bb=bk, m=ml, tgg=tg: e.tensor_copy(
                    out=A4c[:, m, tgg * 512:(tgg + 1) * 512], in_=QB[bb]), inc="v")
            else:
                tok_c = b.emit("s", lambda e, bb=bk, m=ml, tgg=tg: e.activation(
                    out=A4c[:, m, tgg * 512:(tgg + 1) * 512], in_=QB[bb], func=AF.Copy), inc="s")
            bank_free[bk] = tok_c

    tok_qp = None
    last_q_mm = None
    for m in range(KD):
        for tg in range(4):
            bk = ci % 4
            ci += 1
            b.waits("t", [yn_evs[tg * 4 + 3], tok_sq2])
            b.wait("t", bank_free[bk])
            tmm = None
            for k in range(KD):
                tmm = b.emit("t", lambda e, kk=k, mm=m, tgg=tg, bb=bk, st=(k == 0), sp=(k == KD - 1): e.matmul(
                    out=QB[bb], lhsT=N2a[:, 4096 + kk * 1024 + mm * 128:4096 + kk * 1024 + (mm + 1) * 128],
                    rhs=N3y[:, kk, tgg * 512:(tgg + 1) * 512], start=st, stop=sp),
                    inc="t" if k == KD - 1 else None) or tmm
            last_q_mm = tmm
            eng = "v" if ci % 2 == 0 else "s"
            b.wait(eng, tmm)
            if eng == "v":
                tok_qp = b.emit("v", lambda e, bb=bk, mm=m, tgg=tg: e.tensor_copy(
                    out=N1v[:, mm, tgg * 512:(tgg + 1) * 512], in_=QB[bb]), inc="v")
            else:
                tok_qp = b.emit("s", lambda e, bb=bk, mm=m, tgg=tg: e.activation(
                    out=N1v[:, mm, tgg * 512:(tgg + 1) * 512], in_=QB[bb], func=AF.Copy), inc="s")
            bank_free[bk] = tok_qp

    tok_kp = None
    for m in range(KD):
        for tg in range(4):
            bk = ci % 4
            ci += 1
            b.waits("t", [tok_c, tok_uk, last_q_mm])
            b.wait("t", bank_free[bk])
            tmm = None
            for kl in range(KL):
                tmm = b.emit("t", lambda e, kk=kl, mm=m, tgg=tg, bb=bk, st=(kl == 0), sp=(kl == KL - 1): e.matmul(
                    out=QB[bb], lhsT=N2a[:, 12288 + kk * 1024 + mm * 128:12288 + kk * 1024 + (mm + 1) * 128],
                    rhs=A4c[:, kk, tgg * 512:(tgg + 1) * 512], start=st, stop=sp),
                    inc="t" if kl == KL - 1 else None) or tmm
            last_k_mm = tmm
            b.wait("v", tmm)
            tok_kp = b.emit("v", lambda e, bb=bk, mm=m, tgg=tg: e.tensor_copy(
                out=N3y[:, mm, tgg * 512:(tgg + 1) * 512], in_=QB[bb]), inc="v")
            bank_free[bk] = tok_kp

    vON = N2a.rearrange("p (t x) -> p t x", x=1040)
    b.wait("v", last_k_mm)
    tok_von = b.emit("v", lambda e: e.memset(
        vON.rearrange("p t (h c) -> p t h c", c=65)[:, :, :, 64:65], 1.0), inc="v")
    WSr = WSb.ap().rearrange("p (k n) -> p k n", n=1024)
    tok_vp = None
    last_v_mm = None
    for tt in range(NT):
        for half in range(2):
            bk = ci % 4
            ci += 1
            b.waits("t", [tok_c, tok_uv, tok_von, last_k_mm])
            b.wait("t", bank_free[bk])
            tmm = None
            for kl in range(KL):
                tmm = b.emit("t", lambda e, kk=kl, t_=tt, hf=half, bb=bk, st=(kl == 0), sp=(kl == KL - 1): e.matmul(
                    out=QB[bb], lhsT=A4c[:, kk, t_ * 128:(t_ + 1) * 128],
                    rhs=WSr[:, kk, hf * 512:(hf + 1) * 512], start=st, stop=sp),
                    inc="t" if kl == KL - 1 else None) or tmm
            last_v_mm = tmm
            b.wait("v", tmm)
            vdst = vON[:, tt].rearrange("p (h c) -> p h c", c=65)[:, half * 8:(half + 1) * 8, 0:64]
            vsrc = QB[bk].rearrange("p (h c) -> p h c", c=64)
            tok_vp = b.emit("v", lambda e, d=vdst, s=vsrc: e.tensor_copy(out=d, in_=s), inc="v")
            bank_free[bk] = tok_vp

    tok_wo2 = dma("g", "dWo2", A4r, wd["fWo"].rearrange("(k p) n -> p k n", p=128),
                  waits=[last_v_mm])

    QP = [qpad.ap()[:, 0:512], qpad.ap()[:, 512:1024]]

    def v_slice(h, kc):
        return N2a[:, kc * 1040 + h * 65:kc * 1040 + (h + 1) * 65]

    def later(a, c):
        if a is None:
            return c
        if c is None:
            return a
        return a if a[1] >= c[1] else c

    PT = [PTb.ap()[:, 0:1024], PTb.ap()[:, 1024:2048]]
    pt_free = [tok_num_last, tok_num_last]
    tok_qpz = b.emit("v", lambda e: e.memset(qpad.ap(), 0.0), inc="v")
    qp_free = [tok_qpz, tok_qpz]
    MSK = [WSb.ap()[:, t * 512:(t + 1) * 512] for t in range(4)]
    b.wait("g", last_v_mm)
    tok_mm0 = b.emit("g", lambda e: e.memset(WSb.ap()[:, 0:2048], 1.0), inc="g")
    b.wait("g", tok_mm0, force=True)
    tok_msk = None
    for t in range(4):
        tok_msk = b.emit("g", lambda e, tt=t: e.affine_select(
            out=MSK[tt], in_=MSK[tt], pattern=[[1, 512]], compare_op=ALU.is_ge,
            fill=0.0, base=-128 * tt, channel_multiplier=-1), inc="g")
    quad_free = [later(bank_free[0], bank_free[1]), later(bank_free[2], bank_free[3])]
    tail_guard = []
    tok_accg = [None]
    last_tsq = [b1_tsq]
    last_zstt = [None]

    def emit_zstage(qj, qt, ztok):
        ms = mslot_ap(qt)
        zev, zmm = transpose_tile(ms, N1v[:, 0:8, qt * 128:(qt + 1) * 128], [ztok])
        tail_toks.append(zev)
        tail_guard.append(zev)
        bklg = 4 + (qj % 2)
        b.waits("t", [zev, tok_gw])
        b.wait("t", bank_free[bklg])
        tlg = None
        for k in range(KD):
            tlg = b.emit("t", lambda e, kk=k, tt=qt, pb=qj % 2, st_=(k == 0), sp=(k == KD - 1): e.matmul(
                out=SB[pb][:, 0:8], lhsT=N1v[:, kk, tt * 128:(tt + 1) * 128],
                rhs=gwb.ap()[:, kk * 8:(kk + 1) * 8], start=st_, stop=sp),
                inc="t" if k == KD - 1 else None) or tlg
        b.wait("v", tlg)
        tstash = b.emit("v", lambda e, tt=qt, pb=qj % 2: e.tensor_copy(
            out=gts.ap()[:, tt * 8:(tt + 1) * 8], in_=SB[pb][:, 0:8]), inc="v")
        bank_free[bklg] = tstash
        tail_toks.append(tstash)
    tok_av_final = None
    last_scores = None
    tz = None
    tail_toks = []

    def mslot_ap(qt):
        return (stg1.ap()[:, (qt % 4) * 1024:(qt % 4) * 1024 + 1024] if qt % 4 < 2
                else stg2.ap()[:, (qt % 4 - 2) * 1024:(qt % 4 - 2) * 1024 + 1024])

    def emit_qpad_copy(h, qg):
        par = h % 2
        b.wait("v", qp_free[par])
        b.wait("v", tok_qp)
        return b.emit("v", lambda e, hh=h, g=qg, pr=par: e.tensor_copy(
            out=QP[pr][64 * pr:64 * pr + 64, :],
            in_=N1.ap()[64 * pr:64 * pr + 64,
                        (hh // 2) * 2048 + g * 512:(hh // 2) * 2048 + (g + 1) * 512]), inc="v")

    for qg in range(4):
        nkc = 4 * qg + 4
        nkcg = (nkc + 1) // 2
        av_evac_toks = []
        qc_tok = {}
        if qg == 0:
            qc_tok[0] = emit_qpad_copy(0, qg)
            qc_tok[1] = emit_qpad_copy(1, qg)
        else:
            qc_tok[0], qc_tok[1] = nxt0, nxt1
        for h in range(H):
            par = h % 2
            bkav = 4 + par
            pav = SB[par]
            tqc = qc_tok[h]
            av_first = [True]
            tav = None
            pend = None

            def emit_avs(kcg, tzk, lastgrp):
                nonlocal tav
                b.wait("t", tzk)
                if av_first[0]:
                    b.wait("t", bank_free[bkav])
                nkc_here = min(2, nkc - kcg * 2)
                for j in range(nkc_here):
                    kc = kcg * 2 + j
                    for qj in range(4):
                        qt = qg * 4 + qj
                        if kc > qt:
                            continue
                        last = lastgrp and (kc == nkc - 1 and qj == 3)
                        tav = b.emit("t", lambda e, hh=h, kk=kc, qq=qj, pb=kcg % 2, jj=j, pa=pav,
                                     st=av_first[0], sp=last: e.matmul(
                            out=pa[:, qq * 65:(qq + 1) * 65],
                            lhsT=PT[pb][:, jj * 512 + qq * 128:jj * 512 + (qq + 1) * 128],
                            rhs=v_slice(hh, kk), start=st, stop=sp),
                            inc="t" if last else None) or tav
                        av_first[0] = False
                pt_free[kcg % 2] = tav

            for kcg in range(nkcg):
                qd = kcg % 2
                sv = PSQ.ap()[:, qd * 1024:(qd + 1) * 1024]
                b.waits("t", [tok_qp, tok_kp, tok_vp, tqc])
                b.wait("t", quad_free[qd])
                toks_sc = None
                nkc_here = min(2, nkc - kcg * 2)
                for j in range(nkc_here):
                    kc = kcg * 2 + j
                    toks_sc = b.emit("t", lambda e, hh=h, kk=kc, s=sv, jj=j, pr=par: e.matmul(
                        out=s[:, jj * 512:(jj + 1) * 512],
                        lhsT=N3y[:, hh // 2, kk * 128:(kk + 1) * 128],
                        rhs=QP[pr], start=True, stop=True), inc="t")
                    last_scores = toks_sc
                    qp_free[par] = toks_sc
                if kcg == nkcg - 1 and h + 2 < H:
                    qc_tok[h + 2] = emit_qpad_copy(h + 2, qg)
                ncols = nkc_here * 512
                b.wait("s", toks_sc)
                b.wait("s", pt_free[kcg % 2])
                texp = b.emit("s", lambda e, s=sv, n=ncols, pb=kcg % 2: e.activation(
                    out=PT[pb][:, 0:n], in_=s[:, 0:n], func=AF.Exp, scale=0.125), inc="s")
                quad_free[qd] = texp
                tz = texp
                for j in range(nkc_here):
                    kc = kcg * 2 + j
                    if kc < 4 * qg:
                        continue
                    b.wait("v", texp)
                    b.wait("v", tok_msk)
                    tz = b.emit("v", lambda e, kk=kc, g=qg, pb=kcg % 2, jj=j: e.tensor_mul(
                        out=PT[pb][:, jj * 512:(jj + 1) * 512],
                        in0=PT[pb][:, jj * 512:(jj + 1) * 512], in1=MSK[kk - 4 * g]), inc="v")
                if pend is not None:
                    emit_avs(pend[0], pend[1], False)
                pend = (kcg, tz)
            emit_avs(pend[0], pend[1], True)
            tok_av_final = tav
            b.wait("v", tav)
            den = pav[:, 0:260].rearrange("p (j c) -> p j c", c=65)[:, :, 64:65]
            st4 = stats.ap()[:, 8:12].rearrange("p (a c) -> p a c", c=1)
            b.emit("v", lambda e, d=den, o=st4: e.tensor_copy(out=o, in_=d))
            b.emit("v", lambda e: e.reciprocal(out=stats.ap()[:, 12:16], in_=stats.ap()[:, 8:12]))
            tev = None
            b.waits("v", tail_guard)
            for half2 in range(2):
                stg = stg1 if half2 == 0 else stg2
                a0, a1 = broadcast_tensor_aps(
                    pav[:, half2 * 130:half2 * 130 + 130].rearrange("p (j c) -> p j c", c=65)[:, :, 0:64],
                    stats.ap()[:, 12 + 2 * half2:14 + 2 * half2].rearrange("p (j c) -> p j c", c=1))
                dst = stg.ap()[:, 0:2048].rearrange("p (q c) -> p q c", c=1024)[:, :, h * 64:h * 64 + 64]
                tev = b.emit("v", lambda e, x=a0, y=a1, d=dst: e.tensor_tensor(
                    out=d, in0=x, in1=y, op=ALU.mult), inc="v" if half2 == 1 else None) or tev
            bank_free[bkav] = tev
            av_evac_toks.append(tev)

        if qg < 3:
            nxt0 = emit_qpad_copy(0, qg + 1)
            nxt1 = emit_qpad_copy(1, qg + 1)
        tail_toks = []
        tail_guard = []
        st = stats.ap()
        mla_ev = []
        for qj in range(4):
            ms = mslot_ap(qg * 4 + qj)
            ev, _ = transpose_tile(ms, ms.rearrange("p (k n) -> p k n", n=128), av_evac_toks)
            mla_ev.append(ev)
        ztoks = []
        pend_z = None
        for qj in range(4):
            qt = qg * 4 + qj
            ms = mslot_ap(qt)
            ms_r = ms.rearrange("p (k n) -> p k n", n=128)
            acc_t = accT.ap()[:, qt * 1024:(qt + 1) * 1024]
            y_done = []
            tgc2 = [None, None]
            for half in range(2):
                b.wait("g", last_tsq[0])
                b.wait("g", last_zstt[0])
                if tok_b1c[0] is not None:
                    b.wait("g", tok_b1c[0])
                tgc2[half] = b.emit("g", lambda e, a=acc_t, hf=half, tt=qt: e.tensor_scalar(
                    out=scrF.ap()[:, hf * 512:(hf + 1) * 512], in0=a[:, hf * 512:(hf + 1) * 512],
                    scalar1=stats.ap()[:, 16 + tt:17 + tt], scalar2=stats.ap()[:, 32 + tt:33 + tt],
                    op0=ALU.subtract, op1=ALU.mult), inc="g")
            for half in range(2):
                b.waits("t", [mla_ev[qj], tok_wo2])
                b.wait("t", bank_free[7])
                tw = None
                for k in range(KD):
                    tw = b.emit("t", lambda e, kk=k, hf=half, mr=ms_r, st_=(k == 0), sp=(k == KD - 1): e.matmul(
                        out=S7, lhsT=mr[:, kk, :],
                        rhs=A4r[:, kk, hf * 512:(hf + 1) * 512], start=st_, stop=sp),
                        inc="t" if k == KD - 1 else None) or tw
                b.waits("v", [tw, tgc2[half]])
                ty1 = b.emit("v", lambda e, hf=half: e.tensor_add(
                    out=scrF.ap()[:, hf * 512:(hf + 1) * 512], in0=S7[:, 0:512],
                    in1=scrF.ap()[:, hf * 512:(hf + 1) * 512]), inc="v")
                bank_free[7] = ty1
                y_done.append(ty1)
                b.wait("g", ty1)
                tok_accg[0] = b.emit("g", lambda e, a=acc_t, hf=half: e.tensor_add(
                    out=a[:, hf * 512:(hf + 1) * 512], in0=a[:, hf * 512:(hf + 1) * 512],
                    in1=scrF.ap()[:, hf * 512:(hf + 1) * 512]), inc="g")
            if pend_z is not None:
                emit_zstage(*pend_z)
            for half in range(2):
                b.waits("s", y_done)
                tsq = b.emit("s", lambda e, hf=half, m_=ms: e.activation(
                    out=m_[:, 0:512], in_=scrF.ap()[:, hf * 512:(hf + 1) * 512], func=AF.Square,
                    accum_out=st[:, hf:hf + 1]), inc="s")
                b.wait("v", tsq)
                last_tsq[0] = tsq
            tms = b.emit("v", lambda e: e.tensor_add(out=st[:, 0:1], in0=st[:, 0:1], in1=st[:, 1:2]), inc="v")
            b.wait("s", tms)
            tsd = b.emit("s", lambda e: e.activation(out=st[:, 2:3], in_=st[:, 0:1], func=AF.Sqrt,
                                                     bias=EPS_RMS, scale=1.0 / 1024), inc="s")
            b.wait("v", tsd)
            b.emit("v", lambda e: e.reciprocal(out=st[:, 2:3], in_=st[:, 2:3]))
            b.wait("v", tok_g2)
            tzt = b.emit("v", lambda e, m_=ms: e.scalar_tensor_tensor(
                out=m_, in0=scrF.ap(), scalar=stats.ap()[:, 2:3], in1=g2b.ap(),
                op0=ALU.mult, op1=ALU.mult), inc="v")
            ztoks.append(tzt)
            last_zstt[0] = tzt
            pend_z = (qj, qt, tzt)
        emit_zstage(*pend_z)
    tok_z = list(tail_toks)
    tok_mask_last = tz
    bank_free[0] = bank_free[1] = quad_free[0]
    bank_free[2] = bank_free[3] = quad_free[1]

    if upto == "B":
        b.waits("y", tok_z)
        dma("y", "dOut", out.rearrange("(t p) c -> p t c", p=128),
            accT.ap().rearrange("p (t c) -> p t c", c=1024))
        dma("y", "dOut", dbg["zdbg"], N1.ap(), waits=tok_z)
        tok_g3 = dma("y", "dOut", dbg["gdbg"], gts.ap(), waits=tok_z)
        b.wait("y", tok_g3)
        return finish(nc, b)

    W2S = [N3.ap()[:, 0:8192].rearrange("p (k n) -> p k n", n=1024),
           N3.ap()[:, 8192:16384].rearrange("p (k n) -> p k n", n=1024),
           A4.ap().rearrange("p (k n) -> p k n", n=1024)]
    w2r_half = wd["w2"].rearrange("e (hh kh p) n -> e hh p kh n", p=128, kh=8)
    w1slot = [WSb.ap()[:, i * 1024:(i + 1) * 1024] for i in range(4)]
    w1sem = ["dW1a", "dW1b", "dW1c", "dW1d"]
    w1r = wd["w1"].rearrange("e (kc p) (m col) -> e m p kc col", p=128, col=128)
    w2r = wd["w2"].rearrange("e (kh p) n -> e p kh n", p=128)

    panel_consumer = {}
    pidx = 0
    ws_free = tok_msk
    tok_out_last = None

    slot_free = [last_scores, last_scores, tok_qf]
    slot_free[2] = None
    w2sem = ["dW2a", "dW2b", "dW2c"]
    w2tok = [None, None, None]

    hid_free = tok_av_final
    tok_half0 = None
    last_wo2_mm = tw
    slot_free[2] = last_wo2_mm
    mask_last_use = tok_mask_last

    sgc = scrF.ap()
    LG3 = gts.ap().rearrange("p (q c) -> p q c", c=8)
    eqA, eqA3 = sgc[:, 0:128], sgc[:, 0:128].rearrange("p (q c) -> p q c", c=8)
    m2A, m2A3 = sgc[:, 128:256], sgc[:, 128:256].rearrange("p (q c) -> p q c", c=8)
    mx3 = sgc[:, 256:272].rearrange("p (q c) -> p q c", c=1)
    mx23 = sgc[:, 272:288].rearrange("p (q c) -> p q c", c=1)
    ga3 = sgc[:, 304:320].rearrange("p (q c) -> p q c", c=1)
    gb3 = sgc[:, 320:336].rearrange("p (q c) -> p q c", c=1)
    b.wait("v", tok_accg[0])
    b.emit("v", lambda e: e.reduce_max(out=mx3, in_=LG3, axis=AX.X))
    ba0, ba1 = broadcast_tensor_aps(LG3, mx3)
    b.emit("v", lambda e, x=ba0, y=ba1: e.tensor_tensor(out=eqA3, in0=x, in1=y, op=ALU.is_equal))
    b.emit("v", lambda e: e.scalar_tensor_tensor(out=m2A, in0=eqA, scalar=-1e30, in1=gts.ap(),
                                                 op0=ALU.mult, op1=ALU.add))
    b.emit("v", lambda e: e.reduce_max(out=mx23, in_=m2A3, axis=AX.X))
    tmd = b.emit("v", lambda e: e.tensor_tensor(out=sgc[:, 288:304], in0=sgc[:, 256:272],
                                                in1=sgc[:, 272:288], op=ALU.subtract), inc="v")
    b.wait("s", tmd)
    tsg = b.emit("s", lambda e: e.activation(out=sgc[:, 304:320], in_=sgc[:, 288:304],
                                             func=AF.Sigmoid), inc="s")
    b.wait("v", tsg)
    b.emit("v", lambda e: e.tensor_scalar(out=sgc[:, 320:336], in0=sgc[:, 304:320], scalar1=-1.0,
                                          scalar2=1.0, op0=ALU.mult, op1=ALU.add))
    ba2, ba3 = broadcast_tensor_aps(LG3, mx23)
    b.emit("v", lambda e, x=ba2, y=ba3: e.tensor_tensor(out=m2A3, in0=x, in1=y, op=ALU.is_equal))
    ba4, ba5 = broadcast_tensor_aps(m2A3, gb3)
    b.emit("v", lambda e, x=ba4, y=ba5: e.tensor_tensor(out=m2A3, in0=x, in1=y, op=ALU.mult))
    ba6, ba7 = broadcast_tensor_aps(eqA3, ga3)
    b.emit("v", lambda e, x=ba6, y=ba7: e.tensor_tensor(out=eqA3, in0=x, in1=y, op=ALU.mult))
    b.emit("v", lambda e: e.tensor_add(out=gts.ap(), in0=eqA, in1=m2A), inc="v")

    for e in range(NE):
        s_lo, s_hi = (2 * e) % 3, (2 * e + 1) % 3

        def emit_w2(e=e, s_lo=s_lo, s_hi=s_hi):
            for part, sl in ((0, s_lo), (1, s_hi)):
                w = [slot_free[sl]] if slot_free[sl] is not None else []
                w2tok[sl] = dma("g", w2sem[sl], W2S[sl], w2r_half[e, part], waits=w)
        ye_last = None
        for tg in range(2):
            hid_toks = []
            for m in range(KH):
                slot = pidx % 4
                pw = [panel_consumer.get(pidx - 4, mask_last_use or ws_free)]
                ptok = dma("g", w1sem[slot], w1slot[slot].rearrange("p (kc col) -> p kc col", col=128),
                           w1r[e, m], waits=pw)
                if tg == 0 and m == 3:
                    emit_w2()
                bks = ((m % 2) * 2, (m % 2) * 2 + 1)
                pmm = None
                for half in range(2):
                    bk = bks[half]
                    b.wait("t", ptok)
                    b.wait("t", bank_free[bk])
                    b.waits("t", tok_z)
                    for kc in range(KD):
                        pmm = b.emit("t", lambda e_, kk=kc, hf=half, bb=bk, sl=slot, g=tg,
                                     st=(kc == 0), sp=(kc == KD - 1): e_.matmul(
                            out=QB[bb], lhsT=w1slot[sl][:, kk * 128:(kk + 1) * 128],
                            rhs=N1v[:, kk, g * 1024 + hf * 512:g * 1024 + (hf + 1) * 512],
                            start=st, stop=sp),
                            inc="t" if (kc == KD - 1) else None) or pmm
                panel_consumer[pidx] = pmm
                pidx += 1
                b.wait("s", pmm)
                b.wait("s", hid_free)
                tsl = b.emit("s", lambda e_, mm=m: e_.activation(
                    out=N2a[:, mm * 1024:(mm + 1) * 1024], in_=PSQ.ap()[:, (mm % 2) * 1024:(mm % 2) * 1024 + 1024],
                    func=AF.Silu), inc="s")
                bank_free[bks[0]] = bank_free[bks[1]] = tsl
                hid_toks.append(tsl)
            YEB = {4: SB[0], 5: SB[1], 7: S7}
            yeb = [4, 5, 7]
            for tt in range(8):
                qt = tg * 8 + tt
                pair = (yeb[(2 * tt) % 3], yeb[(2 * tt + 1) % 3])
                b.wait("t", bank_free[pair[0]])
                b.wait("t", bank_free[pair[1]])
                b.waits("t", [w2tok[s_lo], w2tok[s_hi]])
                tye = [None, None]
                for kh in range(KH):
                    b.wait("t", hid_toks[kh])
                    ws2 = W2S[s_lo if kh < 8 else s_hi]
                    for half in range(2):
                        tye[half] = b.emit("t", lambda e_, kk=kh, hf=half, t_=tt, pp=pair[half], w2a=ws2,
                                     st=(kh == 0), sp=(kh == KH - 1): e_.matmul(
                            out=YEB[pp][:, 0:512],
                            lhsT=N2a[:, kk * 1024 + t_ * 128:kk * 1024 + t_ * 128 + 128],
                            rhs=w2a[:, kk % 8, hf * 512:(hf + 1) * 512], start=st, stop=sp),
                            inc="t" if (kh == KH - 1) else None) or tye[half]
                ye_last = tye[1]
                for half in range(2):
                    b.wait("v", tye[half])
                    b.wait("v", tok_accg[0])
                    dstp = accT.ap()[:, qt * 1024 + half * 512:qt * 1024 + (half + 1) * 512]
                    tst = b.emit("v", lambda e_, pp=pair[half], d=dstp, tt_=qt, ee=e: e_.scalar_tensor_tensor(
                        out=d, in0=YEB[pp][:, 0:512], scalar=gts.ap()[:, tt_ * 8 + ee:tt_ * 8 + ee + 1],
                        in1=d, op0=ALU.mult, op1=ALU.add), inc="v" if half == 1 else None)
                    bank_free[pair[half]] = tst
                    tok_out_last = tst
                    if e == NE - 1 and tg == 0 and tt == 7 and half == 1:
                        tok_half0 = tst
            hid_free = ye_last
        slot_free[s_lo] = ye_last
        slot_free[s_hi] = ye_last

    b.wait("y", tok_half0)
    dma("y", "dOut", out.rearrange("(t p) c -> p t c", p=128)[:, 0:8, :],
        accT.ap().rearrange("p (t c) -> p t c", c=1024)[:, 0:8, :])
    b.wait("y", tok_out_last)
    tok_out = dma("y", "dOut", out.rearrange("(t p) c -> p t c", p=128)[:, 8:16, :],
                  accT.ap().rearrange("p (t c) -> p t c", c=1024)[:, 8:16, :])
    b.wait("y", tok_out)
    return finish(nc, b)


def finish(nc, b):
    semnames = sorted(b.cnt.keys())
    stack = ExitStack()
    sems = {n: stack.enter_context(nc.semaphore(n)) for n in semnames}
    with nc.Block() as blk:
        blk.tensor(lambda e: b.replay(e, "t", sems))
        blk.vector(lambda e: b.replay(e, "v", sems))
        blk.scalar(lambda e: b.replay(e, "s", sems))
        blk.gpsimd(lambda e: b.replay(e, "g", sems))
        blk.sync(lambda e: b.replay(e, "y", sems))
    stack.close()
    return nc


NAMES = {
    "aWq": "L0_attn_Wq", "aWk": "L0_attn_Wk", "aWv": "L0_attn_Wv", "aWo": "L0_attn_Wo",
    "g1": "L0_ffn_g1", "fWq": "L0_ffn_Wq", "fWdkv": "L0_ffn_Wdkv", "fWuk": "L0_ffn_Wuk",
    "fWuv": "L0_ffn_Wuv", "fWo": "L0_ffn_Wo", "g2": "L0_ffn_g2", "gw": "L0_ffn_gate_w",
    "w1": "L0_ffn_w1", "w2": "L0_ffn_w2",
}


def run(upto, inputs, trace=False):
    nc = build(upto)
    x = np.ascontiguousarray(np.asarray(inputs["L0_attn_x"], dtype=np.float32))
    wmap = {k: np.ascontiguousarray(np.asarray(inputs[v], dtype=np.float32)) for k, v in NAMES.items()}
    in_maps = [dict(wmap, x=x[i]) for i in range(8)]
    return run_bass_kernel_spmd(nc, in_maps, core_ids=list(range(8)), trace=trace)


def kernel(**inputs):
    res = run("full", inputs)
    return np.stack([res.results[i]["out"] for i in range(8)], axis=0)


# version 33 (final)
# speedup vs baseline: 1.0408x; 1.0394x over previous
import numpy as np
from contextlib import ExitStack
import concourse.bass as bass
import concourse.mybir as mybir
from concourse.bass import broadcast_tensor_aps
from concourse.bass_utils import run_bass_kernel_spmd

F32 = mybir.dt.float32
BF16 = mybir.dt.bfloat16
AF = mybir.ActivationFunctionType
ALU = mybir.AluOpType
AX = mybir.AxisListType

S, D, H, DH, LAT, NE, ED = 2048, 1024, 16, 64, 512, 8, 2048
NT = S // 128
KD = D // 128
KL = LAT // 128
KH = ED // 128
EPS_LN, EPS_PF, EPS_RMS = 1e-5, 1e-6, 1e-6

ENGS = ("t", "v", "s", "g", "y")


class Bld:

    def __init__(self):
        self.ops = {e: [] for e in ENGS}
        self.cnt = {e: 0 for e in ENGS}
        self.waited = {e: {} for e in ENGS}

    AUTOCHAIN = ("v", "s", "g")

    def emit(self, e, maker, inc=None, n=1):
        if e in self.AUTOCHAIN and inc is None:
            inc, n = e, 1
        if e in self.AUTOCHAIN and inc == e:
            prev = self.cnt.get(e, 0)
            if prev > 0:
                self.wait(e, (e, prev), force=True)
        tok = None
        if inc is not None:
            self.cnt[inc] = self.cnt.get(inc, 0) + n
            tok = (inc, self.cnt[inc])
        self.ops[e].append((maker, inc, n))
        return tok

    def wait(self, e, tok, force=False):
        if tok is None:
            return
        sem, val = tok
        if sem == e and not force:
            return
        if self.waited[e].get(sem, 0) >= val:
            return
        self.waited[e][sem] = val
        self.ops[e].append((("w", sem, val), None, 0))

    def waits(self, e, toks):
        for t in toks:
            self.wait(e, t)

    def replay(self, eng, name, sems):
        for maker, inc, n in self.ops[name]:
            if isinstance(maker, tuple):
                eng.wait_ge(sems[maker[1]], maker[2])
            else:
                ins = maker(eng)
                if inc is not None:
                    ins.then_inc(sems[inc], n)


def build(upto="full"):
    nc = bass.Bass("TRN2", target_bir_lowering=False)

    xin = nc.dram_tensor("x", [S, D], F32, kind="ExternalInput").ap()
    out = nc.dram_tensor("out", [S, D], F32, kind="ExternalOutput").ap()
    wd = {}
    for name, shape in [
        ("aWq", [D, D]), ("aWk", [D, D]), ("aWv", [D, D]), ("aWo", [D, D]),
        ("g1", [D]), ("fWq", [D, D]), ("fWdkv", [D, LAT]), ("fWuk", [LAT, D]),
        ("fWuv", [LAT, D]), ("fWo", [D, D]), ("g2", [D]), ("gw", [D, NE]),
        ("w1", [NE, D, ED]), ("w2", [NE, ED, D]),
    ]:
        wd[name] = nc.dram_tensor(name, shape, F32, kind="ExternalInput").ap()

    dbg = {}
    if upto == "B":
        dbg["zdbg"] = nc.dram_tensor("zdbg", [128, 16384], BF16, kind="ExternalOutput").ap()
        dbg["gdbg"] = nc.dram_tensor("gdbg", [128, 128], F32, kind="ExternalOutput").ap()

    accT = nc.alloc_sbuf_tensor("accT", [128, NT * 1024], F32)
    N1 = nc.alloc_sbuf_tensor("N1", [128, 16384], BF16)
    N2 = nc.alloc_sbuf_tensor("N2", [128, 16640], BF16)
    N3 = nc.alloc_sbuf_tensor("N3", [128, 16384], BF16)
    A4 = nc.alloc_sbuf_tensor("A4", [128, 8192], BF16)
    WSb = nc.alloc_sbuf_tensor("WSb", [128, 4096], BF16)
    stg1 = nc.alloc_sbuf_tensor("stg1", [128, 2048], BF16)
    stg2 = nc.alloc_sbuf_tensor("stg2", [128, 2080], BF16)
    PTb = nc.alloc_sbuf_tensor("PTb", [128, 2048], BF16)
    g2b = nc.alloc_sbuf_tensor("g2b", [128, 1024], BF16)
    g1k = nc.alloc_sbuf_tensor("g1k", [128, 8], F32)
    gwb = nc.alloc_sbuf_tensor("gwb", [128, 64], BF16)
    idn = nc.alloc_sbuf_tensor("idn", [128, 128], BF16)
    scrF = nc.alloc_sbuf_tensor("scrF", [128, 1024], F32)
    qpad = nc.alloc_sbuf_tensor("qpad", [128, 1024], BF16)
    stats = nc.alloc_sbuf_tensor("stats", [128, 72], F32)
    gts = nc.alloc_sbuf_tensor("gts", [128, 128], F32)
    cst = nc.alloc_sbuf_tensor("cst", [128, 8], F32)

    CONSTS = [EPS_LN, EPS_RMS, 0.0, 1.0, -1.0, 1.0 / 1024, -1e30, EPS_PF]
    for i, v in enumerate(CONSTS):
        nc.const_aps.register(F32, v, cst.ap()[:, i:i + 1])

    PSQ = nc.alloc_psum_tensor("PSQ", [128, 2048], F32)
    PS4 = nc.alloc_psum_tensor("PS4", [128, 512], F32)
    PS5 = nc.alloc_psum_tensor("PS5", [128, 512], F32)
    PST = nc.alloc_psum_tensor("PST", [128, 1024], BF16)
    PS7 = nc.alloc_psum_tensor("PS7", [128, 512], F32)
    QB = [PSQ.ap()[:, i * 512:(i + 1) * 512] for i in range(4)]
    SB = [PS4.ap(), PS5.ap()]
    S7 = PS7.ap()
    PSTa = PST.ap()
    PSTr = PST.ap().rearrange("p (k n) -> p k n", n=128)

    b = Bld()
    bank_free = [None] * 8

    def mk_memsets(aps_vals):
        def f(e):
            ins = None
            for a, v in aps_vals:
                ins = e.memset(a, v)
            return ins
        return f

    tok_cst = b.emit("v", mk_memsets([(cst.ap()[:, i:i + 1], v) for i, v in enumerate(CONSTS)]), inc="v")
    b.wait("s", tok_cst)
    tok_ms = b.emit("g", lambda e: e.memset(idn.ap(), 1.0), inc="g")
    b.wait("g", tok_ms, force=True)
    tok_idn = b.emit("g", lambda e: e.affine_select(
        out=idn.ap(), in_=idn.ap(), pattern=[[1, 128]],
        compare_op=ALU.is_equal, fill=0.0, base=0, channel_multiplier=-1), inc="g")
    tok_vones = []
    for half in range(2):
        ap = stg2.ap()[:, half * 1040:half * 1040 + 1040].rearrange(
            "p (h c) -> p h c", c=65)[:, :, 64:65]
        tok_vones.append(b.emit("v", lambda e, a=ap: e.memset(a, 1.0), inc="v"))

    def dma(eng, sem, dst, src, waits=(), slow=False):
        b.waits(eng, waits)
        return b.emit(eng, lambda e, d=dst, s=src, sl=slow: e.dma_start(
            out=d, in_=s, allow_slow_non_contiguous=sl), inc=sem, n=16)

    tok_xc = []
    for xc in range(4):
        tok_xc.append(dma("y", "dX%d" % xc,
                      accT.ap().rearrange("p (t c) -> p t c", c=1024)[:, xc * 4:(xc + 1) * 4, :],
                      xin.rearrange("(t p) c -> p t c", p=128)[:, xc * 4:(xc + 1) * 4, :]))
    tok_x = tok_xc[-1]

    N3r = N3.ap().rearrange("p (k n) -> p k n", n=1024)
    A4r = A4.ap()[:, 0:8192].rearrange("p (k n) -> p k n", n=1024)
    tok_wk = dma("g", "dWk", N3r[:, 0:8, :], wd["aWk"].rearrange("(k p) n -> p k n", p=128))
    tok_wv = dma("g", "dWv", N3r[:, 8:16, :], wd["aWv"].rearrange("(k p) n -> p k n", p=128))
    tok_wq = dma("g", "dWq", A4r, wd["aWq"].rearrange("(k p) n -> p k n", p=128))
    tok_g1 = dma("g", "dG1", g1k.ap(), wd["g1"].rearrange("(k p) -> p k", p=128), slow=True)
    tok_g2 = dma("g", "dG2", g2b.ap(),
                 wd["g2"].rearrange("(a n) -> a n", a=1).partition_broadcast(128))
    tok_gw = dma("g", "dGw", gwb.ap().rearrange("p (k n) -> p k n", n=8),
                 wd["gw"].rearrange("(k p) n -> p k n", p=128))
    tok_uv = dma("g", "dWuv", WSb.ap().rearrange("p (k n) -> p k n", n=1024),
                 wd["fWuv"].rearrange("(k p) n -> p k n", p=128))

    ln_guard = [None]

    def ln_stats(src_ap, wait_t=None):
        st = stats.ap()
        b.emit("v", lambda e: e.reduce_sum(out=st[:, 0:1], in_=src_ap, axis=AX.X))
        b.emit("v", lambda e: e.tensor_scalar_mul(out=st[:, 1:2], in0=st[:, 0:1], scalar1=1.0 / 1024))
        b.wait("s", ln_guard[0])
        if wait_t is not None:
            b.wait("s", wait_t)
        tsq = b.emit("s", lambda e: e.activation(out=scrF.ap(), in_=src_ap, func=AF.Square,
                                                 accum_out=st[:, 2:3]), inc="s")
        b.wait("v", tsq)
        b.emit("v", lambda e: e.tensor_scalar_mul(out=st[:, 2:3], in0=st[:, 2:3], scalar1=1.0 / 1024))
        b.emit("v", lambda e: e.tensor_mul(out=st[:, 3:4], in0=st[:, 1:2], in1=st[:, 1:2]))
        tv = b.emit("v", lambda e: e.tensor_sub(out=st[:, 3:4], in0=st[:, 2:3], in1=st[:, 3:4]), inc="v")
        b.wait("s", tv)
        b.wait("s", tok_cst)
        tsd = b.emit("s", lambda e: e.activation(out=st[:, 4:5], in_=st[:, 3:4], func=AF.Sqrt, bias=EPS_LN), inc="s")
        b.wait("v", tsd)
        trs = b.emit("v", lambda e: e.reciprocal(out=st[:, 5:6], in_=st[:, 4:5]), inc="v")
        ln_guard[0] = trs
        return trs

    def transpose_tile(src16, dst_ap, waits):
        b.waits("t", waits)
        b.wait("t", bank_free[6])
        tok = None
        for c in range(8):
            tok = b.emit("t", lambda e, cc=c: e.matmul(
                out=PSTa[:, cc * 128:(cc + 1) * 128], lhsT=src16[:, cc * 128:(cc + 1) * 128],
                rhs=idn.ap(), is_transpose=True, start=(cc == 0), stop=(cc == 7)),
                inc="t" if c == 7 else None)
        b.wait("v", tok)
        ev = b.emit("v", lambda e, d=dst_ap: e.tensor_copy(out=d, in_=PSTr), inc="v")
        bank_free[6] = ev
        return ev, tok

    N1v = N1.ap().rearrange("p (k n) -> p k n", n=2048)

    b.wait("t", tok_idn)
    scr_guard = [None, None]
    slot_guard = [None, None]
    tok_aFM = None
    st = stats.ap()
    b.wait("s", tok_cst)
    pend_tr = None
    for c in range(4):
        lo = 4 * c
        b.wait("v", tok_xc[c])
        b.wait("s", tok_xc[c])
        tsq = None
        for t in range(lo, lo + 4):
            acc_t = accT.ap()[:, t * 1024:(t + 1) * 1024]
            b.emit("s", lambda e, a=acc_t, tt=t: e.activation(
                out=scrF.ap(), in_=a, func=AF.Copy, accum_out=st[:, 16 + tt:17 + tt]))
            tsq = b.emit("s", lambda e, a=acc_t, tt=t: e.activation(
                out=scrF.ap(), in_=a, func=AF.Square, accum_out=st[:, 32 + tt:33 + tt]), inc="s")
        b.wait("v", tsq)
        b.emit("v", lambda e, l=lo: e.tensor_scalar_mul(out=st[:, 16 + l:20 + l], in0=st[:, 16 + l:20 + l], scalar1=1.0 / 1024))
        b.emit("v", lambda e, l=lo: e.tensor_scalar_mul(out=st[:, 32 + l:36 + l], in0=st[:, 32 + l:36 + l], scalar1=1.0 / 1024))
        b.emit("v", lambda e, l=lo: e.tensor_mul(out=st[:, l:l + 4], in0=st[:, 16 + l:20 + l], in1=st[:, 16 + l:20 + l]))
        tv = b.emit("v", lambda e, l=lo: e.tensor_sub(out=st[:, 32 + l:36 + l], in0=st[:, 32 + l:36 + l], in1=st[:, l:l + 4]), inc="v")
        b.wait("s", tv)
        tsd = b.emit("s", lambda e, l=lo: e.activation(out=st[:, 32 + l:36 + l], in_=st[:, 32 + l:36 + l], func=AF.Sqrt, bias=EPS_LN), inc="s")
        b.wait("v", tsd)
        b.emit("v", lambda e, l=lo: e.reciprocal(out=st[:, 32 + l:36 + l], in_=st[:, 32 + l:36 + l]))
        for t in range(lo, lo + 4):
            acc_t = accT.ap()[:, t * 1024:(t + 1) * 1024]
            slot = stg1.ap()[:, (t % 2) * 1024:(t % 2) * 1024 + 1024]
            b.wait("v", slot_guard[t % 2])
            ta = b.emit("v", lambda e, a=acc_t, sl=slot, tt=t: e.tensor_scalar(
                out=sl, in0=a, scalar1=st[:, 16 + tt:17 + tt], scalar2=st[:, 32 + tt:33 + tt],
                op0=ALU.subtract, op1=ALU.mult), inc="v")
            if pend_tr is not None:
                ev, mm = transpose_tile(*pend_tr[0:3])
                slot_guard[pend_tr[3] % 2] = mm
            pend_tr = (slot, N1v[:, 0:8, t * 128:(t + 1) * 128], [ta], t)
    ev, mm = transpose_tile(*pend_tr[0:3])
    tok_aFM = ev

    KVB = [SB[0], SB[1], S7]

    def kv_view(h):
        return KVB[h // 7][0:64, 65 * (h % 7):65 * (h % 7) + 65]

    kf_slot_free = [slot_guard[0], slot_guard[1]]
    v_slot_free = [tok_vones[0], tok_vones[1]]
    kv_last = None
    last_kproj_mm = None
    pend_kv = None

    def emit_kv(tt, kfa, vpa, tks):
        nonlocal kv_last
        b.waits("t", tks)
        for h in range(H):
            first = (tt == 0 and h % 7 == 0)
            last = (tt == NT - 1 and h in (6, 13, 15))
            kv_last = b.emit("t", lambda e, hh=h, kf_=kfa, vp_=vpa, st_=first, sp=last: e.matmul(
                out=kv_view(hh), lhsT=kf_[:, hh * 64:(hh + 1) * 64],
                rhs=vp_[:, hh * 65:(hh + 1) * 65], start=st_, stop=sp),
                inc="t" if h == H - 1 else None)
        kf_slot_free[tt % 2] = kv_last
        v_slot_free[tt % 2] = kv_last

    for tt in range(NT):
        b.waits("t", [tok_aFM, tok_wk, tok_wv])
        for bk in range(4):
            b.wait("t", bank_free[bk])
        tok_kv = None
        for k in range(KD):
            aT = N1v[:, k, tt * 128:(tt + 1) * 128]
            for j, (rhs, bk) in enumerate([
                (N3r[:, k, 0:512], 0), (N3r[:, k, 512:1024], 1),
                (N3r[:, 8 + k, 0:512], 2), (N3r[:, 8 + k, 512:1024], 3),
            ]):
                inc = "t" if (k == KD - 1 and j == 3) else None
                tok_kv = b.emit("t", lambda e, a=aT, r=rhs, bb=bk, st=(k == 0), sp=(k == KD - 1): e.matmul(
                    out=QB[bb], lhsT=a, rhs=r, start=st, stop=sp), inc=inc) or tok_kv
        last_kproj_mm = tok_kv
        if pend_kv is not None:
            emit_kv(*pend_kv)
            pend_kv = None
        sl = tt % 2
        kf = stg1.ap()[:, sl * 1024:sl * 1024 + 1024]
        vp = stg2.ap()[:, sl * 1040:sl * 1040 + 1040]
        b.wait("v", tok_kv)
        b.wait("v", kf_slot_free[sl])
        kpsum = PSQ.ap()[:, 0:1024]
        b.wait("v", scr_guard[0])
        b.emit("v", lambda e, kp=kpsum, d=kf: e.tensor_scalar_max(out=d[:, 0:512], in0=kp[:, 0:512], scalar1=0.0))
        tmin0 = b.emit("v", lambda e, kp=kpsum: e.tensor_scalar_min(out=scrF.ap()[:, 0:512], in0=kp[:, 0:512], scalar1=0.0), inc="v")
        b.emit("v", lambda e, kp=kpsum, d=kf: e.tensor_scalar_max(out=d[:, 512:1024], in0=kp[:, 512:1024], scalar1=0.0))
        tmin = b.emit("v", lambda e, kp=kpsum: e.tensor_scalar_min(out=scrF.ap()[:, 512:1024], in0=kp[:, 512:1024], scalar1=0.0), inc="v")
        vdst = vp.rearrange("p (h c) -> p h c", c=65)[:, :, 0:64]
        vsrc = PSQ.ap()[:, 1024:2048].rearrange("p (h c) -> p h c", c=64)
        b.wait("s", tok_kv)
        b.wait("s", v_slot_free[sl])
        tv = b.emit("s", lambda e, d=vdst, s_=vsrc: e.activation(out=d, in_=s_, func=AF.Copy), inc="s")
        b.wait("s", tmin)
        b.wait("s", scr_guard[0])
        te = b.emit("s", lambda e: e.activation(out=scrF.ap(), in_=scrF.ap(), func=AF.Exp), inc="s")
        bank_free[0] = tmin0
        bank_free[1] = tmin
        bank_free[2] = bank_free[3] = tv
        b.wait("g", te)
        tk = b.emit("g", lambda e, d=kf: e.tensor_add(out=d, in0=d, in1=scrF.ap()), inc="g")
        scr_guard[0] = tk
        pend_kv = (tt, kf, vp, [tk, tv])
    emit_kv(*pend_kv)
    kvS = PTb.ap()
    kstg = stg1.ap()[0:64, 0:1040]
    b.wait("v", kv_last)
    b.emit("v", lambda e: e.tensor_copy(out=kstg[:, 0:455], in_=SB[0][0:64, 0:455]))
    b.emit("v", lambda e: e.tensor_copy(out=kstg[:, 455:910], in_=SB[1][0:64, 0:455]))
    b.emit("v", lambda e: e.tensor_copy(out=kstg[:, 910:1040], in_=S7[0:64, 0:130]))
    tok_kvSv = b.emit("v", lambda e: e.memset(kvS[:, 0:1040], 0.0), inc="v")
    bank_free[4] = bank_free[5] = bank_free[7] = tok_kvSv
    bd = lambda a: a.rearrange("p (j c) -> p j c", c=130)
    dma("y", "dKV", bd(kvS[0:64, 0:1040])[:, :, 0:65],
        bd(kstg)[:, :, 0:65], waits=[tok_kvSv])
    tok_kvS = dma("y", "dKV", bd(kvS[64:128, 0:1040])[:, :, 65:130],
                  bd(kstg)[:, :, 65:130])

    if upto == "A2":
        b.waits("y", [tok_kvS])
        tok_out = dma("y", "dOut", out.rearrange("(t p) c -> p t c", p=128),
                      accT.ap().rearrange("p (t c) -> p t c", c=1024))
        b.wait("y", tok_out)
        return finish(nc, b)

    tok_wo = dma("g", "dWo", N3r[:, 0:8, :], wd["aWo"].rearrange("(k p) n -> p k n", p=128),
                 waits=[last_kproj_mm])

    N2q = N2.ap()[:, 0:16384].rearrange("p (m n) -> p m n", n=2048)
    tok_qf = None
    last_qproj_mm = None
    qi = 0
    for m in range(KD):
        for tg in range(4):
            bk = qi % 4
            qi += 1
            b.waits("t", [tok_wq, tok_aFM])
            b.wait("t", bank_free[bk])
            tmm = None
            for k in range(KD):
                tmm = b.emit("t", lambda e, kk=k, mm=m, tgg=tg, bb=bk, st=(k == 0), sp=(k == KD - 1): e.matmul(
                    out=QB[bb], lhsT=A4r[:, kk, mm * 128:(mm + 1) * 128],
                    rhs=N1v[:, kk, tgg * 512:(tgg + 1) * 512], start=st, stop=sp),
                    inc="t" if k == KD - 1 else None) or tmm
            last_qproj_mm = tmm
            b.wait("v", tmm)
            sc = scrF.ap()[:, (qi % 2) * 512:(qi % 2) * 512 + 512]
            dst = N2q[:, m, tg * 512:(tg + 1) * 512]
            b.emit("v", lambda e, bb=bk, d=dst: e.tensor_scalar_max(out=d, in0=QB[bb], scalar1=0.0))
            b.wait("v", scr_guard[qi % 2])
            tmin = b.emit("v", lambda e, bb=bk, s=sc: e.tensor_scalar_min(out=s, in0=QB[bb], scalar1=0.0), inc="v")
            b.wait("s", tmin)
            b.wait("s", scr_guard[qi % 2])
            te = b.emit("s", lambda e, s=sc: e.activation(out=s, in_=s, func=AF.Exp), inc="s")
            b.wait("g", te)
            tok_qf = b.emit("g", lambda e, d=dst, s=sc: e.tensor_add(out=d, in0=d, in1=s), inc="g")
            bank_free[bk] = tmin
            scr_guard[qi % 2] = tok_qf

    if upto == "A3":
        b.waits("y", [tok_qf])
        tok_out = dma("y", "dOut", out.rearrange("(t p) c -> p t c", p=128),
                      accT.ap().rearrange("p (t c) -> p t c", c=1024))
        b.wait("y", tok_out)
        return finish(nc, b)



    wo_done = []
    tok_num_last = None
    pend_a4 = None
    prev_scale_tok = [None]

    def emit_a4_tail(qt, attn, attn_toks):
        nonlocal wo_done, b1_tsq
        afm = stg2.ap()[:, 0:1024]
        afm_r = afm.rearrange("p (k n) -> p k n", n=128)
        ev, _ = transpose_tile(attn, afm_r, attn_toks)
        evs = [ev]
        for half in range(2):
            bk = (qt % 2) * 2 + half
            b.waits("t", evs + [tok_wo])
            b.wait("t", bank_free[bk])
            tw = None
            for k in range(KD):
                tw = b.emit("t", lambda e, kk=k, hf=half, bb=bk, ar=afm_r, st=(k == 0), sp=(k == KD - 1): e.matmul(
                    out=QB[bb], lhsT=ar[:, kk, :], rhs=N3r[:, kk, hf * 512:(hf + 1) * 512],
                    start=st, stop=sp), inc="t" if k == KD - 1 else None) or tw
            b.wait("v", tw)
            dstp = accT.ap()[:, qt * 1024 + half * 512:qt * 1024 + (half + 1) * 512]
            tr1 = b.emit("v", lambda e, bb=bk, d=dstp: e.tensor_add(out=d, in0=QB[bb], in1=d), inc="v")
            bank_free[bk] = tr1
            wo_done.append(tr1)
        a_t = accT.ap()[:, qt * 1024:(qt + 1) * 1024]
        stp = stats.ap()
        b.emit("v", lambda e, a=a_t, tt=qt: e.reduce_sum(out=stp[:, 16 + tt:17 + tt], in_=a, axis=AX.X))
        b.wait("s", scr_guard[0])
        b.wait("s", scr_guard[1])
        b.wait("s", tr1)
        b1_tsq = b.emit("s", lambda e, a=a_t, tt=qt: e.activation(
            out=scrF.ap(), in_=a, func=AF.Square, accum_out=stp[:, 32 + tt:33 + tt]), inc="s")
        if qt == NT - 1:
            emit_b1_chain()

    def emit_b1_chain():
        st = stats.ap()
        b.wait("v", b1_tsq)
        b.emit("v", lambda e: e.tensor_scalar_mul(out=st[:, 16:32], in0=st[:, 16:32], scalar1=1.0 / 1024))
        b.emit("v", lambda e: e.tensor_scalar_mul(out=st[:, 32:48], in0=st[:, 32:48], scalar1=1.0 / 1024))
        b.emit("v", lambda e: e.tensor_mul(out=st[:, 0:16], in0=st[:, 16:32], in1=st[:, 16:32]))
        tv = b.emit("v", lambda e: e.tensor_sub(out=st[:, 32:48], in0=st[:, 32:48], in1=st[:, 0:16]), inc="v")
        b.wait("s", tv)
        tsd = b.emit("s", lambda e: e.activation(out=st[:, 48:64], in_=st[:, 32:48], func=AF.Sqrt, bias=EPS_LN), inc="s")
        b.wait("v", tsd)
        b.emit("v", lambda e: e.reciprocal(out=st[:, 48:64], in_=st[:, 48:64]))
        b.emit("v", lambda e: e.tensor_mul(out=st[:, 0:16], in0=st[:, 32:48], in1=st[:, 48:64]))
        b.emit("v", lambda e: e.tensor_mul(out=st[:, 0:16], in0=st[:, 0:16], in1=st[:, 48:64]))
        tv2 = b.emit("v", lambda e: e.tensor_copy(out=st[:, 32:48], in_=st[:, 48:64]), inc="v")
        b.wait("s", tv2)
        tsd2 = b.emit("s", lambda e: e.activation(out=st[:, 0:16], in_=st[:, 0:16], func=AF.Sqrt, bias=EPS_RMS), inc="s")
        b.wait("v", tsd2)
        b.emit("v", lambda e: e.reciprocal(out=st[:, 0:16], in_=st[:, 0:16]))
        b.emit("v", lambda e: e.tensor_mul(out=st[:, 0:16], in0=st[:, 0:16], in1=st[:, 32:48]))

    b1_tsq = None
    NUMB = {4: SB[0], 5: SB[1], 7: S7}
    numbks = [4, 5, 7]
    for qt in range(NT):
        attn = stg1.ap()[:, (qt % 2) * 1024:(qt % 2) * 1024 + 1024]
        attn_toks = []
        for hg in range(4):
            bk = numbks[(qt * 4 + hg) % 3]
            pv = NUMB[bk]
            b.wait("t", tok_qf)
            b.wait("t", tok_kvS)
            b.wait("t", bank_free[bk])
            tnum = None
            for pj in range(2):
                pair = hg * 2 + pj
                tnum = b.emit("t", lambda e, pp=pair, jj=pj, q0=qt * 128, pvv=pv, st=(pj == 0), sp=(pj == 1): e.matmul(
                    out=pvv[:, jj * 130:(jj + 1) * 130], lhsT=N2q[:, pp, q0:q0 + 128],
                    rhs=kvS[:, pp * 130:(pp + 1) * 130], start=st, stop=sp), inc="t" if pj == 1 else None) or tnum
            tok_num_last = tnum
            if hg == 0 and pend_a4 is not None:
                emit_a4_tail(*pend_a4)
                pend_a4 = None
            b.wait("v", tnum)
            den = pv[:, 0:260].rearrange("p (j c) -> p j c", c=65)[:, :, 64:65]
            st4 = stats.ap()[:, 8:12].rearrange("p (a c) -> p a c", c=1)
            b.emit("v", lambda e, d=den, o=st4: e.tensor_scalar_add(out=o, in0=d, scalar1=EPS_PF))
            b.emit("v", lambda e: e.reciprocal(out=stats.ap()[:, 12:16], in_=stats.ap()[:, 8:12]))
            a0, a1 = broadcast_tensor_aps(
                pv[:, 0:260].rearrange("p (j c) -> p j c", c=65)[:, :, 0:64],
                stats.ap()[:, 12:16].rearrange("p (j c) -> p j c", c=1))
            tok_a = b.emit("v", lambda e, hg_=hg, at=attn, x=a0, y=a1: e.tensor_tensor(
                out=at[:, hg_ * 256:(hg_ + 1) * 256].rearrange("p (j c) -> p j c", c=64),
                in0=x, in1=y, op=ALU.mult), inc="v")
            bank_free[bk] = tok_a
            attn_toks.append(tok_a)
        pend_a4 = (qt, attn, attn_toks)
    emit_a4_tail(*pend_a4)
    tok_r1 = wo_done[-1]

    if upto == "A":
        b.waits("y", wo_done)
        tok_out = dma("y", "dOut", out.rearrange("(t p) c -> p t c", p=128),
                      accT.ap().rearrange("p (t c) -> p t c", c=1024))
        b.wait("y", tok_out)
        return finish(nc, b)

    N2a = N2.ap()
    tok_dkv = dma("g", "dWdkv", N2a[:, 0:4096].rearrange("p (k n) -> p k n", n=512),
                  wd["fWdkv"].rearrange("(k p) n -> p k n", p=128), waits=[tok_num_last])
    tok_q2 = dma("g", "dWq2", N2a[:, 4096:12288].rearrange("p (k n) -> p k n", n=1024),
                 wd["fWq"].rearrange("(k p) n -> p k n", p=128), waits=[tok_num_last])
    tok_uk = dma("g", "dWuk", N2a[:, 12288:16384].rearrange("p (k n) -> p k n", n=1024),
                 wd["fWuk"].rearrange("(k p) n -> p k n", p=128), waits=[tok_num_last])
    b.waits("v", [tok_dkv, tok_g1])
    tok_sdkv = None
    for k in range(KD):
        ap = N2a[:, k * 512:(k + 1) * 512]
        tok_sdkv = b.emit("v", lambda e, a=ap, kk=k: e.tensor_scalar_mul(
            out=a, in0=a, scalar1=g1k.ap()[:, kk:kk + 1]), inc="v" if k == KD - 1 else None) or tok_sdkv
    b.wait("v", tok_q2)
    tok_sq2 = None
    for k in range(KD):
        ap = N2a[:, 4096 + k * 1024:4096 + (k + 1) * 1024]
        tok_sq2 = b.emit("v", lambda e, a=ap, kk=k: e.tensor_scalar_mul(
            out=a, in0=a, scalar1=g1k.ap()[:, kk:kk + 1]), inc="v" if k == KD - 1 else None) or tok_sq2

    N3y = N3.ap().rearrange("p (k n) -> p k n", n=2048)
    st = stats.ap()
    b.wait("v", tok_r1)
    slot_guard = [None, None]
    tok_yn = None
    pend_tr = None
    yn_evs = [None] * NT
    for t in range(NT):
        acc_t = accT.ap()[:, t * 1024:(t + 1) * 1024]
        slot = stg1.ap()[:, (t % 2) * 1024:(t % 2) * 1024 + 1024]
        b.wait("v", slot_guard[t % 2])
        b.wait("v", kf_slot_free[t % 2])
        ty = b.emit("v", lambda e, a=acc_t, sl=slot, tt=t: e.tensor_scalar(
            out=sl, in0=a, scalar1=st[:, 16 + tt:17 + tt], scalar2=st[:, 0 + tt:1 + tt],
            op0=ALU.subtract, op1=ALU.mult), inc="v")
        if pend_tr is not None:
            ev, mm = transpose_tile(*pend_tr[0:3])
            slot_guard[pend_tr[3] % 2] = mm
            yn_evs[pend_tr[3]] = ev
        pend_tr = (slot, N3y[:, 0:8, t * 128:(t + 1) * 128], [ty], t)
    ev, mm = transpose_tile(*pend_tr[0:3])
    yn_evs[NT - 1] = ev
    tok_yn = ev

    A4c = A4.ap().rearrange("p (l n) -> p l n", n=2048)
    ci = 0
    tok_c = None
    for ml in range(KL):
        for tg in range(4):
            bk = ci % 4
            ci += 1
            b.waits("t", [yn_evs[tg * 4 + 3], tok_sdkv])
            b.wait("t", bank_free[bk])
            tmm = None
            for k in range(KD):
                tmm = b.emit("t", lambda e, kk=k, m=ml, tgg=tg, bb=bk, st=(k == 0), sp=(k == KD - 1): e.matmul(
                    out=QB[bb], lhsT=N2a[:, kk * 512 + m * 128:kk * 512 + (m + 1) * 128],
                    rhs=N3y[:, kk, tgg * 512:(tgg + 1) * 512], start=st, stop=sp),
                    inc="t" if k == KD - 1 else None) or tmm
            eng = "v" if ci % 2 == 0 else "s"
            b.wait(eng, tmm)
            if eng == "v":
                tok_c = b.emit("v", lambda e, bb=bk, m=ml, tgg=tg: e.tensor_copy(
                    out=A4c[:, m, tgg * 512:(tgg + 1) * 512], in_=QB[bb]), inc="v")
            else:
                tok_c = b.emit("s", lambda e, bb=bk, m=ml, tgg=tg: e.activation(
                    out=A4c[:, m, tgg * 512:(tgg + 1) * 512], in_=QB[bb], func=AF.Copy), inc="s")
            bank_free[bk] = tok_c

    tok_qp = None
    last_q_mm = None
    for m in range(KD):
        for tg in range(4):
            bk = ci % 4
            ci += 1
            b.waits("t", [yn_evs[tg * 4 + 3], tok_sq2])
            b.wait("t", bank_free[bk])
            tmm = None
            for k in range(KD):
                tmm = b.emit("t", lambda e, kk=k, mm=m, tgg=tg, bb=bk, st=(k == 0), sp=(k == KD - 1): e.matmul(
                    out=QB[bb], lhsT=N2a[:, 4096 + kk * 1024 + mm * 128:4096 + kk * 1024 + (mm + 1) * 128],
                    rhs=N3y[:, kk, tgg * 512:(tgg + 1) * 512], start=st, stop=sp),
                    inc="t" if k == KD - 1 else None) or tmm
            last_q_mm = tmm
            eng = "v" if ci % 2 == 0 else "s"
            b.wait(eng, tmm)
            if eng == "v":
                tok_qp = b.emit("v", lambda e, bb=bk, mm=m, tgg=tg: e.tensor_copy(
                    out=N1v[:, mm, tgg * 512:(tgg + 1) * 512], in_=QB[bb]), inc="v")
            else:
                tok_qp = b.emit("s", lambda e, bb=bk, mm=m, tgg=tg: e.activation(
                    out=N1v[:, mm, tgg * 512:(tgg + 1) * 512], in_=QB[bb], func=AF.Copy), inc="s")
            bank_free[bk] = tok_qp

    tok_kp = None
    for m in range(KD):
        for tg in range(4):
            bk = ci % 4
            ci += 1
            b.waits("t", [tok_c, tok_uk, last_q_mm])
            b.wait("t", bank_free[bk])
            tmm = None
            for kl in range(KL):
                tmm = b.emit("t", lambda e, kk=kl, mm=m, tgg=tg, bb=bk, st=(kl == 0), sp=(kl == KL - 1): e.matmul(
                    out=QB[bb], lhsT=N2a[:, 12288 + kk * 1024 + mm * 128:12288 + kk * 1024 + (mm + 1) * 128],
                    rhs=A4c[:, kk, tgg * 512:(tgg + 1) * 512], start=st, stop=sp),
                    inc="t" if kl == KL - 1 else None) or tmm
            last_k_mm = tmm
            b.wait("v", tmm)
            tok_kp = b.emit("v", lambda e, bb=bk, mm=m, tgg=tg: e.tensor_copy(
                out=N3y[:, mm, tgg * 512:(tgg + 1) * 512], in_=QB[bb]), inc="v")
            bank_free[bk] = tok_kp

    vON = N2a.rearrange("p (t x) -> p t x", x=1040)
    b.wait("v", last_k_mm)
    tok_von = b.emit("v", lambda e: e.memset(
        vON.rearrange("p t (h c) -> p t h c", c=65)[:, :, :, 64:65], 1.0), inc="v")
    WSr = WSb.ap().rearrange("p (k n) -> p k n", n=1024)
    tok_vp = None
    last_v_mm = None
    for tt in range(NT):
        for half in range(2):
            bk = ci % 4
            ci += 1
            b.waits("t", [tok_c, tok_uv, tok_von, last_k_mm])
            b.wait("t", bank_free[bk])
            tmm = None
            for kl in range(KL):
                tmm = b.emit("t", lambda e, kk=kl, t_=tt, hf=half, bb=bk, st=(kl == 0), sp=(kl == KL - 1): e.matmul(
                    out=QB[bb], lhsT=A4c[:, kk, t_ * 128:(t_ + 1) * 128],
                    rhs=WSr[:, kk, hf * 512:(hf + 1) * 512], start=st, stop=sp),
                    inc="t" if kl == KL - 1 else None) or tmm
            last_v_mm = tmm
            b.wait("v", tmm)
            vdst = vON[:, tt].rearrange("p (h c) -> p h c", c=65)[:, half * 8:(half + 1) * 8, 0:64]
            vsrc = QB[bk].rearrange("p (h c) -> p h c", c=64)
            tok_vp = b.emit("v", lambda e, d=vdst, s=vsrc: e.tensor_copy(out=d, in_=s), inc="v")
            bank_free[bk] = tok_vp

    tok_wo2 = dma("g", "dWo2", A4r, wd["fWo"].rearrange("(k p) n -> p k n", p=128),
                  waits=[last_v_mm])

    QP = [qpad.ap()[:, 0:512], qpad.ap()[:, 512:1024]]

    def v_slice(h, kc):
        return N2a[:, kc * 1040 + h * 65:kc * 1040 + (h + 1) * 65]

    def later(a, c):
        if a is None:
            return c
        if c is None:
            return a
        return a if a[1] >= c[1] else c

    PT = [PTb.ap()[:, 0:1024], PTb.ap()[:, 1024:2048]]
    pt_free = [tok_num_last, tok_num_last]
    tok_qpz = b.emit("v", lambda e: e.memset(qpad.ap(), 0.0), inc="v")
    qp_free = [tok_qpz, tok_qpz]
    MSK = [WSb.ap()[:, t * 512:(t + 1) * 512] for t in range(4)]
    b.wait("g", last_v_mm)
    tok_mm0 = b.emit("g", lambda e: e.memset(WSb.ap()[:, 0:2048], 1.0), inc="g")
    b.wait("g", tok_mm0, force=True)
    tok_msk = None
    for t in range(4):
        tok_msk = b.emit("g", lambda e, tt=t: e.affine_select(
            out=MSK[tt], in_=MSK[tt], pattern=[[1, 512]], compare_op=ALU.is_ge,
            fill=0.0, base=-128 * tt, channel_multiplier=-1), inc="g")
    quad_free = [later(bank_free[0], bank_free[1]), later(bank_free[2], bank_free[3])]
    tail_guard = []
    tok_accg = [None]
    tok_av_final = None
    last_scores = None
    tz = None
    tail_toks = []

    def mslot_ap(qt):
        return (stg1.ap()[:, (qt % 4) * 1024:(qt % 4) * 1024 + 1024] if qt % 4 < 2
                else stg2.ap()[:, (qt % 4 - 2) * 1024:(qt % 4 - 2) * 1024 + 1024])

    def emit_qpad_copy(h, qg):
        par = h % 2
        b.wait("v", qp_free[par])
        b.wait("v", tok_qp)
        return b.emit("v", lambda e, hh=h, g=qg, pr=par: e.tensor_copy(
            out=QP[pr][64 * pr:64 * pr + 64, :],
            in_=N1.ap()[64 * pr:64 * pr + 64,
                        (hh // 2) * 2048 + g * 512:(hh // 2) * 2048 + (g + 1) * 512]), inc="v")

    for qg in range(4):
        nkc = 4 * qg + 4
        nkcg = (nkc + 1) // 2
        av_evac_toks = []
        qc_tok = {}
        if qg == 0:
            qc_tok[0] = emit_qpad_copy(0, qg)
            qc_tok[1] = emit_qpad_copy(1, qg)
        else:
            qc_tok[0], qc_tok[1] = nxt0, nxt1
        for h in range(H):
            par = h % 2
            bkav = 4 + par
            pav = SB[par]
            tqc = qc_tok[h]
            av_first = [True]
            tav = None
            pend = None

            def emit_avs(kcg, tzk, lastgrp):
                nonlocal tav
                b.wait("t", tzk)
                if av_first[0]:
                    b.wait("t", bank_free[bkav])
                nkc_here = min(2, nkc - kcg * 2)
                for j in range(nkc_here):
                    kc = kcg * 2 + j
                    for qj in range(4):
                        qt = qg * 4 + qj
                        if kc > qt:
                            continue
                        last = lastgrp and (kc == nkc - 1 and qj == 3)
                        tav = b.emit("t", lambda e, hh=h, kk=kc, qq=qj, pb=kcg % 2, jj=j, pa=pav,
                                     st=av_first[0], sp=last: e.matmul(
                            out=pa[:, qq * 65:(qq + 1) * 65],
                            lhsT=PT[pb][:, jj * 512 + qq * 128:jj * 512 + (qq + 1) * 128],
                            rhs=v_slice(hh, kk), start=st, stop=sp),
                            inc="t" if last else None) or tav
                        av_first[0] = False
                pt_free[kcg % 2] = tav

            for kcg in range(nkcg):
                qd = kcg % 2
                sv = PSQ.ap()[:, qd * 1024:(qd + 1) * 1024]
                b.waits("t", [tok_qp, tok_kp, tok_vp, tqc])
                b.wait("t", quad_free[qd])
                toks_sc = None
                nkc_here = min(2, nkc - kcg * 2)
                for j in range(nkc_here):
                    kc = kcg * 2 + j
                    toks_sc = b.emit("t", lambda e, hh=h, kk=kc, s=sv, jj=j, pr=par: e.matmul(
                        out=s[:, jj * 512:(jj + 1) * 512],
                        lhsT=N3y[:, hh // 2, kk * 128:(kk + 1) * 128],
                        rhs=QP[pr], start=True, stop=True), inc="t")
                    last_scores = toks_sc
                    qp_free[par] = toks_sc
                if kcg == nkcg - 1 and h + 2 < H:
                    qc_tok[h + 2] = emit_qpad_copy(h + 2, qg)
                ncols = nkc_here * 512
                b.wait("s", toks_sc)
                b.wait("s", pt_free[kcg % 2])
                texp = b.emit("s", lambda e, s=sv, n=ncols, pb=kcg % 2: e.activation(
                    out=PT[pb][:, 0:n], in_=s[:, 0:n], func=AF.Exp, scale=0.125), inc="s")
                quad_free[qd] = texp
                tz = texp
                for j in range(nkc_here):
                    kc = kcg * 2 + j
                    if kc < 4 * qg:
                        continue
                    b.wait("v", texp)
                    b.wait("v", tok_msk)
                    tz = b.emit("v", lambda e, kk=kc, g=qg, pb=kcg % 2, jj=j: e.tensor_mul(
                        out=PT[pb][:, jj * 512:(jj + 1) * 512],
                        in0=PT[pb][:, jj * 512:(jj + 1) * 512], in1=MSK[kk - 4 * g]), inc="v")
                if pend is not None:
                    emit_avs(pend[0], pend[1], False)
                pend = (kcg, tz)
            emit_avs(pend[0], pend[1], True)
            tok_av_final = tav
            b.wait("v", tav)
            den = pav[:, 0:260].rearrange("p (j c) -> p j c", c=65)[:, :, 64:65]
            st4 = stats.ap()[:, 8:12].rearrange("p (a c) -> p a c", c=1)
            b.emit("v", lambda e, d=den, o=st4: e.tensor_copy(out=o, in_=d))
            b.emit("v", lambda e: e.reciprocal(out=stats.ap()[:, 12:16], in_=stats.ap()[:, 8:12]))
            tev = None
            b.waits("v", tail_guard)
            for half2 in range(2):
                stg = stg1 if half2 == 0 else stg2
                a0, a1 = broadcast_tensor_aps(
                    pav[:, half2 * 130:half2 * 130 + 130].rearrange("p (j c) -> p j c", c=65)[:, :, 0:64],
                    stats.ap()[:, 12 + 2 * half2:14 + 2 * half2].rearrange("p (j c) -> p j c", c=1))
                dst = stg.ap()[:, 0:2048].rearrange("p (q c) -> p q c", c=1024)[:, :, h * 64:h * 64 + 64]
                tev = b.emit("v", lambda e, x=a0, y=a1, d=dst: e.tensor_tensor(
                    out=d, in0=x, in1=y, op=ALU.mult), inc="v" if half2 == 1 else None) or tev
            bank_free[bkav] = tev
            av_evac_toks.append(tev)

        if qg < 3:
            nxt0 = emit_qpad_copy(0, qg + 1)
            nxt1 = emit_qpad_copy(1, qg + 1)
        tail_toks = []
        tail_guard = []
        st = stats.ap()
        mla_ev = []
        for qj in range(4):
            ms = mslot_ap(qg * 4 + qj)
            ev, _ = transpose_tile(ms, ms.rearrange("p (k n) -> p k n", n=128), av_evac_toks)
            mla_ev.append(ev)
        ztoks = []
        for qj in range(4):
            qt = qg * 4 + qj
            ms = mslot_ap(qt)
            ms_r = ms.rearrange("p (k n) -> p k n", n=128)
            acc_t = accT.ap()[:, qt * 1024:(qt + 1) * 1024]
            y_done = []
            for half in range(2):
                b.waits("t", [mla_ev[qj], tok_wo2])
                b.wait("t", bank_free[7])
                tw = None
                for k in range(KD):
                    tw = b.emit("t", lambda e, kk=k, hf=half, mr=ms_r, st_=(k == 0), sp=(k == KD - 1): e.matmul(
                        out=S7, lhsT=mr[:, kk, :],
                        rhs=A4r[:, kk, hf * 512:(hf + 1) * 512], start=st_, stop=sp),
                        inc="t" if k == KD - 1 else None) or tw
                b.wait("v", tw)
                b.wait("v", tok_accg[0])
                b.emit("v", lambda e, a=acc_t, hf=half, tt=qt: e.tensor_scalar(
                    out=scrF.ap()[:, hf * 512:(hf + 1) * 512], in0=a[:, hf * 512:(hf + 1) * 512],
                    scalar1=stats.ap()[:, 16 + tt:17 + tt], scalar2=stats.ap()[:, 32 + tt:33 + tt],
                    op0=ALU.subtract, op1=ALU.mult))
                ty1 = b.emit("v", lambda e, hf=half: e.tensor_add(
                    out=scrF.ap()[:, hf * 512:(hf + 1) * 512], in0=S7[:, 0:512],
                    in1=scrF.ap()[:, hf * 512:(hf + 1) * 512]), inc="v")
                bank_free[7] = ty1
                y_done.append(ty1)
                b.wait("g", ty1)
                tok_accg[0] = b.emit("g", lambda e, a=acc_t, hf=half: e.tensor_add(
                    out=a[:, hf * 512:(hf + 1) * 512], in0=a[:, hf * 512:(hf + 1) * 512],
                    in1=scrF.ap()[:, hf * 512:(hf + 1) * 512]), inc="g")
            for half in range(2):
                b.waits("s", y_done)
                tsq = b.emit("s", lambda e, hf=half, m_=ms: e.activation(
                    out=m_[:, 0:512], in_=scrF.ap()[:, hf * 512:(hf + 1) * 512], func=AF.Square,
                    accum_out=st[:, hf:hf + 1]), inc="s")
                b.wait("v", tsq)
            tms = b.emit("v", lambda e: e.tensor_add(out=st[:, 0:1], in0=st[:, 0:1], in1=st[:, 1:2]), inc="v")
            b.wait("s", tms)
            tsd = b.emit("s", lambda e: e.activation(out=st[:, 2:3], in_=st[:, 0:1], func=AF.Sqrt,
                                                     bias=EPS_RMS, scale=1.0 / 1024), inc="s")
            b.wait("v", tsd)
            b.emit("v", lambda e: e.reciprocal(out=st[:, 2:3], in_=st[:, 2:3]))
            b.wait("v", tok_g2)
            tzt = b.emit("v", lambda e, m_=ms: e.scalar_tensor_tensor(
                out=m_, in0=scrF.ap(), scalar=stats.ap()[:, 2:3], in1=g2b.ap(),
                op0=ALU.mult, op1=ALU.mult), inc="v")
            ztoks.append(tzt)
        for qj in range(4):
            qt = qg * 4 + qj
            ms = mslot_ap(qt)
            zev, zmm = transpose_tile(ms, N1v[:, 0:8, qt * 128:(qt + 1) * 128], [ztoks[qj]])
            tail_toks.append(zev)
            tail_guard.append(zev)
            bklg = 4 + (qj % 2)
            b.waits("t", [zev, tok_gw])
            b.wait("t", bank_free[bklg])
            tlg = None
            for k in range(KD):
                tlg = b.emit("t", lambda e, kk=k, tt=qt, pb=qj % 2, st_=(k == 0), sp=(k == KD - 1): e.matmul(
                    out=SB[pb][:, 0:8], lhsT=N1v[:, kk, tt * 128:(tt + 1) * 128],
                    rhs=gwb.ap()[:, kk * 8:(kk + 1) * 8], start=st_, stop=sp),
                    inc="t" if k == KD - 1 else None) or tlg
            b.wait("v", tlg)
            tstash = b.emit("v", lambda e, tt=qt, pb=qj % 2: e.tensor_copy(
                out=gts.ap()[:, tt * 8:(tt + 1) * 8], in_=SB[pb][:, 0:8]), inc="v")
            bank_free[bklg] = tstash
            tail_toks.append(tstash)
    tok_z = list(tail_toks)
    tok_mask_last = tz
    bank_free[0] = bank_free[1] = quad_free[0]
    bank_free[2] = bank_free[3] = quad_free[1]

    if upto == "B":
        b.waits("y", tok_z)
        dma("y", "dOut", out.rearrange("(t p) c -> p t c", p=128),
            accT.ap().rearrange("p (t c) -> p t c", c=1024))
        dma("y", "dOut", dbg["zdbg"], N1.ap(), waits=tok_z)
        tok_g3 = dma("y", "dOut", dbg["gdbg"], gts.ap(), waits=tok_z)
        b.wait("y", tok_g3)
        return finish(nc, b)

    W2S = [N3.ap()[:, 0:8192].rearrange("p (k n) -> p k n", n=1024),
           N3.ap()[:, 8192:16384].rearrange("p (k n) -> p k n", n=1024),
           A4.ap().rearrange("p (k n) -> p k n", n=1024)]
    w2r_half = wd["w2"].rearrange("e (hh kh p) n -> e hh p kh n", p=128, kh=8)
    w1slot = [WSb.ap()[:, i * 1024:(i + 1) * 1024] for i in range(4)]
    w1sem = ["dW1a", "dW1b", "dW1c", "dW1d"]
    w1r = wd["w1"].rearrange("e (kc p) (m col) -> e m p kc col", p=128, col=128)
    w2r = wd["w2"].rearrange("e (kh p) n -> e p kh n", p=128)

    panel_consumer = {}
    pidx = 0
    ws_free = tok_msk
    tok_out_last = None

    slot_free = [last_scores, last_scores, tok_qf]
    slot_free[2] = None
    w2sem = ["dW2a", "dW2b", "dW2c"]
    w2tok = [None, None, None]

    hid_free = tok_av_final
    tok_half0 = None
    last_wo2_mm = tw
    slot_free[2] = last_wo2_mm
    mask_last_use = tok_mask_last

    sgc = scrF.ap()
    LG3 = gts.ap().rearrange("p (q c) -> p q c", c=8)
    eqA, eqA3 = sgc[:, 0:128], sgc[:, 0:128].rearrange("p (q c) -> p q c", c=8)
    m2A, m2A3 = sgc[:, 128:256], sgc[:, 128:256].rearrange("p (q c) -> p q c", c=8)
    mx3 = sgc[:, 256:272].rearrange("p (q c) -> p q c", c=1)
    mx23 = sgc[:, 272:288].rearrange("p (q c) -> p q c", c=1)
    ga3 = sgc[:, 304:320].rearrange("p (q c) -> p q c", c=1)
    gb3 = sgc[:, 320:336].rearrange("p (q c) -> p q c", c=1)
    b.wait("v", tok_accg[0])
    b.emit("v", lambda e: e.reduce_max(out=mx3, in_=LG3, axis=AX.X))
    ba0, ba1 = broadcast_tensor_aps(LG3, mx3)
    b.emit("v", lambda e, x=ba0, y=ba1: e.tensor_tensor(out=eqA3, in0=x, in1=y, op=ALU.is_equal))
    b.emit("v", lambda e: e.scalar_tensor_tensor(out=m2A, in0=eqA, scalar=-1e30, in1=gts.ap(),
                                                 op0=ALU.mult, op1=ALU.add))
    b.emit("v", lambda e: e.reduce_max(out=mx23, in_=m2A3, axis=AX.X))
    tmd = b.emit("v", lambda e: e.tensor_tensor(out=sgc[:, 288:304], in0=sgc[:, 256:272],
                                                in1=sgc[:, 272:288], op=ALU.subtract), inc="v")
    b.wait("s", tmd)
    tsg = b.emit("s", lambda e: e.activation(out=sgc[:, 304:320], in_=sgc[:, 288:304],
                                             func=AF.Sigmoid), inc="s")
    b.wait("v", tsg)
    b.emit("v", lambda e: e.tensor_scalar(out=sgc[:, 320:336], in0=sgc[:, 304:320], scalar1=-1.0,
                                          scalar2=1.0, op0=ALU.mult, op1=ALU.add))
    ba2, ba3 = broadcast_tensor_aps(LG3, mx23)
    b.emit("v", lambda e, x=ba2, y=ba3: e.tensor_tensor(out=m2A3, in0=x, in1=y, op=ALU.is_equal))
    ba4, ba5 = broadcast_tensor_aps(m2A3, gb3)
    b.emit("v", lambda e, x=ba4, y=ba5: e.tensor_tensor(out=m2A3, in0=x, in1=y, op=ALU.mult))
    ba6, ba7 = broadcast_tensor_aps(eqA3, ga3)
    b.emit("v", lambda e, x=ba6, y=ba7: e.tensor_tensor(out=eqA3, in0=x, in1=y, op=ALU.mult))
    b.emit("v", lambda e: e.tensor_add(out=gts.ap(), in0=eqA, in1=m2A), inc="v")

    for e in range(NE):
        s_lo, s_hi = (2 * e) % 3, (2 * e + 1) % 3

        def emit_w2_half(ee, part, sl):
            w = [slot_free[sl]] if slot_free[sl] is not None else []
            w2tok[sl] = dma("g", w2sem[sl], W2S[sl], w2r_half[ee, part], waits=w)
        ye_last = None
        for tg in range(2):
            hid_toks = []
            for m in range(KH):
                slot = pidx % 4
                pw = [panel_consumer.get(pidx - 4, mask_last_use or ws_free)]
                ptok = dma("g", w1sem[slot], w1slot[slot].rearrange("p (kc col) -> p kc col", col=128),
                           w1r[e, m], waits=pw)
                if tg == 0 and m == 3:
                    if e == 0:
                        emit_w2_half(e, 0, s_lo)
                    emit_w2_half(e, 1, s_hi)
                if tg == 1 and m == 0 and e + 1 < NE:
                    emit_w2_half(e + 1, 0, (2 * e + 2) % 3)
                bks = ((m % 2) * 2, (m % 2) * 2 + 1)
                pmm = None
                for half in range(2):
                    bk = bks[half]
                    b.wait("t", ptok)
                    b.wait("t", bank_free[bk])
                    b.waits("t", tok_z)
                    for kc in range(KD):
                        pmm = b.emit("t", lambda e_, kk=kc, hf=half, bb=bk, sl=slot, g=tg,
                                     st=(kc == 0), sp=(kc == KD - 1): e_.matmul(
                            out=QB[bb], lhsT=w1slot[sl][:, kk * 128:(kk + 1) * 128],
                            rhs=N1v[:, kk, g * 1024 + hf * 512:g * 1024 + (hf + 1) * 512],
                            start=st, stop=sp),
                            inc="t" if (kc == KD - 1) else None) or pmm
                panel_consumer[pidx] = pmm
                pidx += 1
                b.wait("s", pmm)
                b.wait("s", hid_free)
                tsl = b.emit("s", lambda e_, mm=m: e_.activation(
                    out=N2a[:, mm * 1024:(mm + 1) * 1024], in_=PSQ.ap()[:, (mm % 2) * 1024:(mm % 2) * 1024 + 1024],
                    func=AF.Silu), inc="s")
                bank_free[bks[0]] = bank_free[bks[1]] = tsl
                hid_toks.append(tsl)
            YEB = {4: SB[0], 5: SB[1], 7: S7}
            yeb = [4, 5, 7]
            for tt in range(8):
                qt = tg * 8 + tt
                pair = (yeb[(2 * tt) % 3], yeb[(2 * tt + 1) % 3])
                b.wait("t", bank_free[pair[0]])
                b.wait("t", bank_free[pair[1]])
                b.waits("t", [w2tok[s_lo]])
                tye = [None, None]
                for kh in range(KH):
                    b.wait("t", hid_toks[kh])
                    if kh == 8:
                        b.wait("t", w2tok[s_hi])
                    ws2 = W2S[s_lo if kh < 8 else s_hi]
                    for half in range(2):
                        tye[half] = b.emit("t", lambda e_, kk=kh, hf=half, t_=tt, pp=pair[half], w2a=ws2,
                                     st=(kh == 0), sp=(kh == KH - 1): e_.matmul(
                            out=YEB[pp][:, 0:512],
                            lhsT=N2a[:, kk * 1024 + t_ * 128:kk * 1024 + t_ * 128 + 128],
                            rhs=w2a[:, kk % 8, hf * 512:(hf + 1) * 512], start=st, stop=sp),
                            inc="t" if (kh == KH - 1) else None) or tye[half]
                ye_last = tye[1]
                for half in range(2):
                    b.wait("v", tye[half])
                    b.wait("v", tok_accg[0])
                    dstp = accT.ap()[:, qt * 1024 + half * 512:qt * 1024 + (half + 1) * 512]
                    tst = b.emit("v", lambda e_, pp=pair[half], d=dstp, tt_=qt, ee=e: e_.scalar_tensor_tensor(
                        out=d, in0=YEB[pp][:, 0:512], scalar=gts.ap()[:, tt_ * 8 + ee:tt_ * 8 + ee + 1],
                        in1=d, op0=ALU.mult, op1=ALU.add), inc="v" if half == 1 else None)
                    bank_free[pair[half]] = tst
                    tok_out_last = tst
                    if e == NE - 1 and tg == 0 and tt == 7 and half == 1:
                        tok_half0 = tst
            hid_free = ye_last
        slot_free[s_lo] = ye_last
        slot_free[s_hi] = ye_last

    b.wait("y", tok_half0)
    dma("y", "dOut", out.rearrange("(t p) c -> p t c", p=128)[:, 0:8, :],
        accT.ap().rearrange("p (t c) -> p t c", c=1024)[:, 0:8, :])
    b.wait("y", tok_out_last)
    tok_out = dma("y", "dOut", out.rearrange("(t p) c -> p t c", p=128)[:, 8:16, :],
                  accT.ap().rearrange("p (t c) -> p t c", c=1024)[:, 8:16, :])
    b.wait("y", tok_out)
    return finish(nc, b)


def finish(nc, b):
    semnames = sorted(b.cnt.keys())
    stack = ExitStack()
    sems = {n: stack.enter_context(nc.semaphore(n)) for n in semnames}
    with nc.Block() as blk:
        blk.tensor(lambda e: b.replay(e, "t", sems))
        blk.vector(lambda e: b.replay(e, "v", sems))
        blk.scalar(lambda e: b.replay(e, "s", sems))
        blk.gpsimd(lambda e: b.replay(e, "g", sems))
        blk.sync(lambda e: b.replay(e, "y", sems))
    stack.close()
    return nc


NAMES = {
    "aWq": "L0_attn_Wq", "aWk": "L0_attn_Wk", "aWv": "L0_attn_Wv", "aWo": "L0_attn_Wo",
    "g1": "L0_ffn_g1", "fWq": "L0_ffn_Wq", "fWdkv": "L0_ffn_Wdkv", "fWuk": "L0_ffn_Wuk",
    "fWuv": "L0_ffn_Wuv", "fWo": "L0_ffn_Wo", "g2": "L0_ffn_g2", "gw": "L0_ffn_gate_w",
    "w1": "L0_ffn_w1", "w2": "L0_ffn_w2",
}


def run(upto, inputs, trace=False):
    nc = build(upto)
    x = np.ascontiguousarray(np.asarray(inputs["L0_attn_x"], dtype=np.float32))
    wmap = {k: np.ascontiguousarray(np.asarray(inputs[v], dtype=np.float32)) for k, v in NAMES.items()}
    in_maps = [dict(wmap, x=x[i]) for i in range(8)]
    return run_bass_kernel_spmd(nc, in_maps, core_ids=list(range(8)), trace=trace)


def kernel(**inputs):
    res = run("full", inputs)
    return np.stack([res.results[i]["out"] for i in range(8)], axis=0)
